# version 18
# baseline (speedup 1.0000x reference)
import numpy as np
import ml_dtypes
import concourse.bass as bass
import concourse.mybir as mybir
from concourse.bass_utils import run_bass_kernel_spmd
from contextlib import ExitStack

F32 = mybir.dt.float32
BF16 = mybir.dt.bfloat16
I8 = mybir.dt.int8
I32 = mybir.dt.int32
AF = mybir.ActivationFunctionType
ALU = mybir.AluOpType
BF = ml_dtypes.bfloat16

ROLES = ("sy", "gp", "ve", "te", "sc")
GROUPS = [[0, 1, 2, 3], [4, 5, 6, 7]]


class Cfg:
    def __init__(self, T_seq=8192, D=2048, INTER=8192):
        self.T_seq = T_seq
        self.To = T_seq // 4
        self.D = D
        self.INTER = INTER
        self.heads = D // 128
        self.Hc = self.heads // 4
        self.n_csub = D // 128
        self.n_fc = 2 * self.heads
        self.n_strip = self.To // 128
        self.QW = min(1024, self.To)
        self.ND = self.QW // 128
        self.FW = self.Hc * 128
        self.n_qc = T_seq // self.QW
        self.n_fh = D // 512
        self.n_tc = self.To // 128
        self.n_tt = self.To // 512
        self.n_ic = INTER // 128
        self.n_oh = D // 512
        self.n_grp = INTER // 512
        self.scale = 1.0 / float(np.sqrt(128.0))
        self.eps = 1e-5


class Sem:
    __slots__ = ("h", "v", "name")

    def __init__(self, h, name):
        self.h = h
        self.v = 0
        self.name = name


class DPair:

    def __init__(self, slist, per=16):
        self.s = tuple(slist)
        self.n = len(self.s)
        self.per = per

    def add(self, instr, i):
        s = self.s[i % self.n]
        s.v += self.per
        if instr is not None:
            instr.then_inc(s.h, self.per)

    def wait(self, eng, i):
        if i < 0:
            return
        s = self.s[i % self.n]
        eng.wait_ge(s.h, self.per * (i // self.n + 1))

    def sems(self):
        return list(self.s)


def build(cfg, dbg=0, upto=99):
    nc = bass.Bass("TRN2", target_bir_lowering=False)
    T, D, INTER, To, Hc = cfg.T_seq, cfg.D, cfg.INTER, cfg.To, cfg.Hc
    NCS = cfg.n_csub
    NK = D // 512

    d_x = nc.dram_tensor("x", [To, D], F32, kind="ExternalInput").ap()
    d_gix = nc.dram_tensor("gix", [1, 1], I32, kind="ExternalInput").ap()
    d_wqk = nc.dram_tensor("wqk", [cfg.n_fc, 128, D], F32, kind="ExternalInput").ap()
    d_bqk = nc.dram_tensor("bqk", [128, cfg.n_fc], F32, kind="ExternalInput").ap()
    d_wv = nc.dram_tensor("wv", [NCS, 128, D], F32, kind="ExternalInput").ap()
    d_bv = nc.dram_tensor("bv", [1, D], F32, kind="ExternalInput").ap()
    d_wo = nc.dram_tensor("wo", [NCS, 128, D], F32, kind="ExternalInput").ap()
    d_bo = nc.dram_tensor("bo", [1, D], F32, kind="ExternalInput").ap()
    d_wgu = nc.dram_tensor("wgu", [cfg.n_ic, 128, 2 * D], I8, kind="ExternalInput").ap()
    d_wd = nc.dram_tensor("wd", [cfg.n_grp, 128, 4, D], I8, kind="ExternalInput").ap()
    d_sg = nc.dram_tensor("sg", [128, cfg.n_ic], F32, kind="ExternalInput").ap()
    d_su = nc.dram_tensor("su", [128, cfg.n_ic], F32, kind="ExternalInput").ap()
    d_sd = nc.dram_tensor("sd", [1, D], F32, kind="ExternalInput").ap()
    d_cos = nc.dram_tensor("cosP", [128, To], BF16, kind="ExternalInput").ap()
    d_sin = nc.dram_tensor("sinP", [128, To], BF16, kind="ExternalInput").ap()
    d_mask = nc.dram_tensor("maskP", [128, cfg.QW + (cfg.ND - 1) * 128], BF16, kind="ExternalInput").ap()
    d_id = nc.dram_tensor("identF", [128, 128], F32, kind="ExternalInput").ap()
    d_y = nc.dram_tensor("y", [To, D], F32, kind="ExternalOutput").ap()

    qk_contrib = nc.dram_tensor("qk_contrib", [2 * D, To], BF16)
    k_gath = nc.dram_tensor("k_gath", [4 * D, To], BF16)
    q_gath = nc.dram_tensor("q_gath", [4 * D, To], BF16)
    v_contrib = nc.dram_tensor("v_contrib", [To, D], BF16)
    v_gath = nc.dram_tensor("v_gath", [T, D], BF16)
    o_contrib = nc.dram_tensor("o_contrib", [Hc * 128, T], BF16)
    o_gath = nc.dram_tensor("o_gath", [4 * Hc * 128, T], BF16)
    x2_sp = nc.dram_tensor("x2_sp", [To, D], F32)
    xln3T_sp = nc.dram_tensor("xln3T_sp", [D, To], BF16)

    dbg_pairs = []
    if dbg >= 1:
        for nm, srct in (("dbg_qkc", qk_contrib), ("dbg_vc", v_contrib)):
            dbg_pairs.append((nc.dram_tensor(nm, list(srct.shape), srct.dtype, kind="ExternalOutput").ap(), srct))
    if dbg >= 2:
        for nm, srct in (("dbg_oc", o_contrib), ("dbg_x2", x2_sp), ("dbg_x3", xln3T_sp)):
            dbg_pairs.append((nc.dram_tensor(nm, list(srct.shape), srct.dtype, kind="ExternalOutput").ap(), srct))

    with ExitStack() as st0:
        sb0 = lambda n, s, d: st0.enter_context(nc.sbuf_tensor(n, s, d))
        sems = {}

        def sem(name):
            if name not in sems:
                sems[name] = Sem(st0.enter_context(nc.semaphore(name)), name)
            return sems[name]

        def dpair(name, per=16, n=2):
            return DPair([sem(name + str(j)) for j in range(n)], per)

        psS = st0.enter_context(nc.psum_tensor("psS", [128, 2048], F32))
        psB = st0.enter_context(nc.psum_tensor("psB", [128, 1024], F32))
        psC = st0.enter_context(nc.psum_tensor("psC", [128, 1024], F32))
        _slots = [(psS, 0), (psS, 512), (psS, 1024), (psS, 1536),
                  (psB, 0), (psB, 512), (psC, 0), (psC, 512)]

        def slot(i):
            t, off = _slots[i % 8]
            return t[:, off:off + 512]

        _q4 = [(psS, 0), (psS, 1024), (psB, 0), (psC, 0)]

        ones_c = sb0("ones_c", [128, 1], BF16)
        ones_r = sb0("ones_r", [1, 128], F32)
        eps_c = sb0("eps_c", [128, 1], F32)
        gix_s = sb0("gix_s", [1, 1], I32)
        bqk_s = sb0("bqk_s", [128, cfg.n_fc], F32)
        sg_s = sb0("sg_s", [128, cfg.n_ic], F32)
        su_s = sb0("su_s", [128, cfg.n_ic], F32)
        negM = sb0("negM", [128, max(Hc, 1)], F32)
        row_s = sb0("row_s", [1, D], F32)
        id_s = sb0("id_s", [128, 128], F32)

        block = st0.enter_context(nc.Block())
        sbar = sem("sbar")
        vt = sem("vtick")
        stk = sem("stick")
        bar = {"v": 0}

        def vop(role, eng, fn, relax=False):
            if role == "ve":
                if vt.v and not relax:
                    eng.wait_ge(vt.h, vt.v)
                fn(eng).then_inc(vt.h, 1)
            vt.v += 1
            return vt.v

        def sop(role, eng, fn, relax=False):
            if role == "sc":
                if stk.v and not relax:
                    eng.wait_ge(stk.h, stk.v)
                fn(eng).then_inc(stk.h, 1)
            stk.v += 1
            return stk.v

        def run_phase(fn):
            snap = {k: s.v for k, s in sems.items()}
            finals = []
            for role in ROLES:
                for k, s in sems.items():
                    s.v = snap.get(k, 0)
                eng_deco = {"sy": block.sync, "gp": block.gpsimd, "ve": block.vector,
                            "te": block.tensor, "sc": block.scalar}[role]

                def section(e, _role=role):
                    e.wait_ge(sbar.h, bar["v"])
                    fn(_role, e)
                section.__name__ = f"{fn.__name__}_{role}"
                eng_deco(section)
                finals.append({k: s.v for k, s in sems.items()})
            for f in finals[1:]:
                bad = [k for k in finals[0] if finals[0][k] != f.get(k)]
                assert not bad, f"lockstep divergence in {fn.__name__}: {bad}"
            bar["v"] = sbar.v

        def endp(role, eng, last_map):
            last = last_map.get(role)
            if last is not None:
                if not isinstance(last, (list, tuple)):
                    last = [last]
                flat = []
                for l in last:
                    flat.extend(l.sems() if isinstance(l, DPair) else [l])
                for l in flat:
                    if l.v:
                        eng.wait_ge(l.h, l.v)
            eng.nop().then_inc(sbar.h, 1)
            sbar.v += 5

        def bcast_row(role, eng, drow, dst, pfx):
            s_ld, s_mm = sem(pfx + "ld"), sem(pfx + "mm")
            if role == "sy":
                eng.dma_start(out=row_s[:], in_=drow).then_inc(s_ld.h, 16)
            s_ld.v += 16
            for k in range(NK):
                if role == "te":
                    if k == 0:
                        eng.wait_ge(s_ld.h, s_ld.v)
                    eng.matmul(slot(k)[:, :], ones_r[:], row_s[:, k * 512:(k + 1) * 512],
                               start=True, stop=True).then_inc(s_mm.h, 1)
                s_mm.v += 1
                if role == "sc":
                    eng.wait_ge(s_mm.h, s_mm.v)
                sop(role, eng, lambda e, k=k: e.copy(dst[:, k * 512:(k + 1) * 512], slot(k)[:, :]))
            return stk.v

        dyn = {}

        def ph0(role, eng):
            s0 = sem("p0")
            vop(role, eng, lambda e: e.memset(ones_c[:], 1.0))
            vop(role, eng, lambda e: e.memset(ones_r[:], 1.0))
            vop(role, eng, lambda e: e.memset(eps_c[:], cfg.eps))
            if role == "sy":
                eng.dma_start(out=bqk_s[:], in_=d_bqk).then_inc(s0.h, 16)
                eng.dma_start(out=sg_s[:], in_=d_sg).then_inc(s0.h, 16)
                eng.dma_start(out=su_s[:], in_=d_su).then_inc(s0.h, 16)
                eng.dma_start(out=gix_s[:], in_=d_gix).then_inc(s0.h, 16)
                eng.dma_start(out=id_s[:], in_=d_id).then_inc(s0.h, 16)
            s0.v += 80
            if role == "sy":
                eng.wait_ge(s0.h, s0.v)
                rg = eng.alloc_register("rg")
                eng.reg_load(rg, gix_s[0:1, 0:1])
                gval = eng.snap(rg)
                dyn["gFW"] = gval * cfg.FW
                dyn["gTo"] = gval * To
            endp(role, eng, {"sy": s0, "ve": vt})

        run_phase(ph0)

        with ExitStack() as stA:
            sbA = lambda n, s, d: stA.enter_context(nc.sbuf_tensor(n, s, d))
            xln1T = sbA("xln1T", [128, NCS * To], BF16)
            bv_bc = sbA("bv_bc", [128, D], F32)

            with ExitStack() as stP:
                sbP = lambda n, s, d: stP.enter_context(nc.sbuf_tensor(n, s, d))
                x_st = [sbP(f"x_st{i}", [128, D], F32) for i in range(2)]
                ln_st = [sbP(f"ln_st{i}", [128, D], F32) for i in range(2)]
                bn6 = sbP("bn6", [128, 6 * NK], F32)
                mv2 = sbP("mv2", [128, 2], F32)
                lnv = sbP("lnv", [128, 1], F32)
                rstd = sbP("rstd", [128, 1], F32)
                s_x = dpair("a_x")
                s_tp = sem("a_tp")

                def ph1(role, eng):
                    ln_h, ev_h = [], []
                    for s in range(cfg.n_strip):
                        if role == "sy":
                            if s >= 2:
                                eng.wait_ge(vt.h, ln_h[s - 2])
                            s_x.add(eng.dma_start(out=x_st[s % 2][:], in_=d_x[s * 128:(s + 1) * 128, :]), s)
                        else:
                            s_x.add(None, s)
                        if role == "ve":
                            s_x.wait(eng, s)
                        for k in range(NK):
                            vop(role, eng, lambda e, k=k, s=s: e.bn_stats(
                                bn6[:, 6 * k:6 * (k + 1)], x_st[s % 2][:, 512 * k:512 * (k + 1)]))
                        bn_now = vop(role, eng, lambda e: e.bn_aggr(mv2[:], bn6[:]))
                        if role == "sc":
                            eng.wait_ge(vt.h, bn_now)
                        sop(role, eng, lambda e: e.activation(lnv[:], mv2[:, 1:2], AF.Ln, bias=eps_c[:, 0:1]))
                        rs_now = sop(role, eng, lambda e: e.activation(rstd[:], lnv[:], AF.Exp, scale=-0.5))
                        if role == "ve":
                            eng.wait_ge(stk.h, rs_now)
                            if s >= 2:
                                eng.wait_ge(s_tp.h, s - 1)
                        ln_h.append(vop(role, eng, lambda e, s=s: e.tensor_scalar(
                            ln_st[s % 2][:], x_st[s % 2][:], mv2[:, 0:1], rstd[:, 0:1],
                            ALU.subtract, ALU.mult)))
                        def tp_dst(s, cs):
                            if s % 2 == 0:
                                return psS[:, cs * 128:(cs + 1) * 128]
                            t2 = psB if cs < 8 else psC
                            return t2[:, (cs % 8) * 128:(cs % 8 + 1) * 128]
                        if role == "te":
                            eng.wait_ge(vt.h, ln_h[s])
                            if s >= 2:
                                eng.wait_ge(stk.h, ev_h[s - 2])
                            for cs in range(NCS):
                                mm = eng.transpose(tp_dst(s, cs),
                                                   ln_st[s % 2][:, cs * 128:(cs + 1) * 128], id_s[:])
                                if cs == NCS - 1:
                                    mm.then_inc(s_tp.h, 1)
                        s_tp.v += 1
                        if role == "sc":
                            eng.wait_ge(s_tp.h, s_tp.v)
                        e_now = 0
                        for k in range(NK):
                            if s % 2 == 0:
                                srcv = psS[:, k * 512:(k + 1) * 512]
                            else:
                                t2 = psB if k < 2 else psC
                                srcv = t2[:, (k % 2) * 512:(k % 2 + 1) * 512]
                            srcv = srcv.rearrange("p (c t) -> p c t", t=128)
                            dstv = xln1T[:].rearrange("p (c t) -> p c t", t=To)[:, 4 * k:4 * (k + 1),
                                                                                s * 128:(s + 1) * 128]
                            e_now = sop(role, eng, lambda e, srcv=srcv, dstv=dstv: e.copy(dstv, srcv))
                        ev_h.append(e_now)
                    endp(role, eng, {"sy": s_x, "ve": vt, "te": s_tp, "sc": stk})

                run_phase(ph1)

            if upto >= 2:
                with ExitStack() as stP:
                    sbP = lambda n, s, d: stP.enter_context(nc.sbuf_tensor(n, s, d))
                    cos_s = sbP("cos_s", [128, To], BF16)
                    sin_s = sbP("sin_s", [128, To], BF16)
                    wq_b = [sbP(f"wq_b{i}", [128, D], BF16) for i in range(2)]
                    qk_sb = [sbP(f"qk_sb{i}", [128, To], BF16) for i in range(2)]
                    rop_b = [sbP(f"rop_b{i}", [128, To], BF16) for i in range(2)]
                    swp = sbP("swp", [128, To], BF16)
                    tmp1 = sbP("tmp1", [128, To], BF16)
                    tmp2 = sbP("tmp2", [128, To], BF16)
                    s_cs = sem("b_cs")
                    s_mt = sem("b_mt")
                    s_wq = dpair("b_wq")
                    s_so = dpair("b_so")
                    n_th = To // 512

                    s_cc = sem("d_cc")

                    def ph2(role, eng):
                        qe_h, ro_h = [], []
                        if role == "sy":
                            eng.dma_start(out=cos_s[:], in_=d_cos).then_inc(s_cs.h, 16)
                            eng.dma_start(out=sin_s[:], in_=d_sin).then_inc(s_cs.h, 16)
                        s_cs.v += 32
                        fc_list = list(range(cfg.heads, cfg.n_fc)) + list(range(cfg.heads))
                        for i_p, fc in enumerate(fc_list):
                            if role == "gp":
                                if i_p >= 2:
                                    eng.wait_ge(s_mt.h, (i_p - 1) * n_th)
                                s_wq.add(eng.dma_start(out=wq_b[i_p % 2][:], in_=d_wqk[fc, :, :]), i_p)
                                if i_p == cfg.heads + 2:
                                    for sp in s_so.sems():
                                        eng.wait_ge(sp.h, 16 * (cfg.heads // 2))
                                    eng.collective_compute(
                                        "AllGather", ALU.bypass, replica_groups=GROUPS,
                                        ins=[qk_contrib.ap()[D:2 * D, :].opt()],
                                        outs=[k_gath.ap().opt()]).then_inc(s_cc.h)
                            else:
                                s_wq.add(None, i_p)
                            if i_p == cfg.heads + 2:
                                s_cc.v += 1
                            e_now = 0
                            for th in range(n_th):
                                i = i_p * n_th + th
                                if role == "te":
                                    if th == 0:
                                        s_wq.wait(eng, i_p)
                                        if i_p >= 2:
                                            eng.wait_ge(stk.h, qe_h[i_p - 2])
                                    for cs in range(NCS):
                                        mm = eng.matmul(slot(i)[:, :], wq_b[i_p % 2][:, cs * 128:(cs + 1) * 128],
                                                        xln1T[:, cs * To + th * 512: cs * To + (th + 1) * 512],
                                                        start=(cs == 0), stop=(cs == NCS - 1))
                                    mm.then_inc(s_mt.h, 1)
                                s_mt.v += 1
                                if role == "sc":
                                    eng.wait_ge(s_mt.h, s_mt.v)
                                    if th == 0 and i_p >= 2:
                                        eng.wait_ge(vt.h, ro_h[i_p - 2])
                                e_now = sop(role, eng, lambda e, i=i, fc=fc, i_p=i_p, th=th: e.activation(
                                    qk_sb[i_p % 2][:, th * 512:(th + 1) * 512], slot(i)[:, :],
                                    AF.Identity, bias=bqk_s[:, fc:fc + 1]))
                            qe_h.append(e_now)
                            if role == "ve":
                                eng.wait_ge(stk.h, qe_h[i_p])
                                eng.wait_ge(s_cs.h, 32)
                                s_so.wait(eng, i_p - 2)
                            a = qk_sb[i_p % 2]
                            vop(role, eng, lambda e, a=a: e.tensor_copy(swp[0:64, :], a[64:128, :]))
                            vop(role, eng, lambda e, a=a: e.tensor_copy(swp[64:128, :], a[0:64, :]))
                            vop(role, eng, lambda e, a=a: e.tensor_tensor(tmp1[:], a[:], cos_s[:], ALU.mult))
                            vop(role, eng, lambda e: e.tensor_tensor(tmp2[:], swp[:], sin_s[:], ALU.mult))
                            ro_h.append(vop(role, eng, lambda e, i_p=i_p: e.tensor_tensor(
                                rop_b[i_p % 2][:], tmp1[:], tmp2[:], ALU.add)))
                            if role == "sy":
                                eng.wait_ge(vt.h, ro_h[i_p])
                                s_so.add(eng.dma_start(out=qk_contrib.ap()[fc * 128:(fc + 1) * 128, :],
                                                       in_=rop_b[i_p % 2][:]), i_p)
                            else:
                                s_so.add(None, i_p)
                        endp(role, eng, {"sy": [s_cs, s_so], "gp": s_wq, "ve": vt, "te": s_mt, "sc": stk})

                    run_phase(ph2)

                with ExitStack() as stP:
                    sbP = lambda n, s, d: stP.enter_context(nc.sbuf_tensor(n, s, d))
                    wv_b = [sbP(f"wv_b{i}", [128, NCS * 512], BF16) for i in range(2)]
                    v_eb = [sbP(f"v_eb{i}", [128, 512], BF16) for i in range(2)]
                    s_mv = sem("c_mv")
                    s_wv = dpair("c_wv")
                    s_vso = dpair("c_vso")

                    def ph3(role, eng):
                        vev_h = []
                        if role == "gp":
                            for nmso in ("b_so0", "b_so1"):
                                spx = sem(nmso)
                                eng.wait_ge(spx.h, spx.v)
                            eng.collective_compute(
                                "AllGather", ALU.bypass, replica_groups=GROUPS,
                                ins=[qk_contrib.ap()[0:D, :].opt()],
                                outs=[q_gath.ap().opt()]).then_inc(s_cc.h)
                        s_cc.v += 1
                        bc_done = bcast_row(role, eng, d_bv, bv_bc, "c_b")
                        for fh in range(cfg.n_fh):
                            if role == "gp":
                                if fh >= 2:
                                    eng.wait_ge(s_mv.h, (fh - 1) * cfg.n_tc)
                                s_wv.add(eng.dma_start(
                                    out=wv_b[fh % 2][:].rearrange("p (c f) -> p c f", f=512),
                                    in_=d_wv[:, :, fh * 512:(fh + 1) * 512].rearrange("c p f -> p c f"),
                                ), fh)
                            else:
                                s_wv.add(None, fh)
                            for tc in range(cfg.n_tc):
                                i = fh * cfg.n_tc + tc
                                if role == "te":
                                    if tc == 0:
                                        s_wv.wait(eng, fh)
                                        eng.wait_ge(stk.h, bc_done)
                                    if i >= 7:
                                        eng.wait_ge(vt.h, vev_h[i - 7])
                                    for cs in range(NCS):
                                        mm = eng.matmul(slot(i)[:, :],
                                                        xln1T[:, cs * To + tc * 128: cs * To + (tc + 1) * 128],
                                                        wv_b[fh % 2][:, cs * 512:(cs + 1) * 512],
                                                        start=(cs == 0), stop=(cs == NCS - 1))
                                    mm.then_inc(s_mv.h, 1)
                                s_mv.v += 1
                                if role == "ve":
                                    eng.wait_ge(s_mv.h, s_mv.v)
                                    s_vso.wait(eng, i - 2)
                                vev_h.append(vop(role, eng, lambda e, i=i, fh=fh: e.scalar_tensor_tensor(
                                    v_eb[i % 2][:], slot(i)[:, :], 1.0,
                                    bv_bc[:, fh * 512:(fh + 1) * 512], ALU.mult, ALU.add)))
                                if role == "sy":
                                    eng.wait_ge(vt.h, vev_h[i])
                                    s_vso.add(eng.dma_start(
                                        out=v_contrib.ap()[tc * 128:(tc + 1) * 128, fh * 512:(fh + 1) * 512],
                                        in_=v_eb[i % 2][:]), i)
                                else:
                                    s_vso.add(None, i)
                        endp(role, eng, {"sy": s_vso, "gp": s_wv, "ve": vt, "te": s_mv, "sc": stk})

                    run_phase(ph3)

        if upto >= 4:
            with ExitStack() as stB:
                sbB = lambda n, s, d: stB.enter_context(nc.sbuf_tensor(n, s, d))
                kT = sbB("kT", [128, Hc * T], BF16)
                v_s = sbB("v_s", [128, (T // 128) * cfg.FW], BF16)
                mask_s = sbB("mask_s", [128, cfg.QW + (cfg.ND - 1) * 128], BF16)
                q_f = sbB("q_f", [128, T], BF16)
                pT2 = [sbB(f"pT2_{i}", [128, 2 * cfg.QW], BF16) for i in range(2)]
                o_b = [sbB(f"o_b{i}", [128, cfg.QW], BF16) for i in range(2)]
                zinv = sbB("zinv", [1, cfg.QW], F32)
                zb_sb = sbB("zb_sb", [128, cfg.QW], F32)
                sq_b = [sbB(f"sq_b{i}", [128, cfg.QW], BF16) for i in range(4)]
                un_r = sbB("un_r", [1, 2 * Hc * (T // cfg.QW)], F32)
                mx_r = sbB("mx_r", [1, 2 * Hc], F32)
                s2_r = sbB("s2_r", [1, Hc], F32)

                NU = T // cfg.QW
                s_cc = sem("d_cc")
                s_kv = sem("d_kv")
                s_nm = sem("d_nm")
                s_nb = sem("d_nb")

                def ph4(role, eng):
                    sq_h, nr_h = [], []
                    if role == "gp":
                        for nmso in ("c_vso0", "c_vso1"):
                            spx = sem(nmso)
                            eng.wait_ge(spx.h, spx.v)
                        eng.collective_compute("AllGather", ALU.bypass, replica_groups=GROUPS,
                                               ins=[v_contrib.ap().opt()], outs=[v_gath.ap().opt()]).then_inc(s_cc.h)
                    s_cc.v += 1
                    k3 = k_gath.ap().rearrange("(r q) t -> q r t", q=D)
                    q3 = q_gath.ap().rearrange("(r q) t -> q r t", q=D)
                    if role == "sy":
                        eng.wait_ge(s_cc.h, 1)
                        for h in range(Hc):
                            eng.dma_start(
                                out=kT[:, h * T:(h + 1) * T].rearrange("p (r t) -> p r t", t=To),
                                in_=k3[h * 128:, :, :][bass.ds(dyn["gFW"], 128), :, :],
                            ).then_inc(s_kv.h, 16)
                        eng.dma_start(out=mask_s[:], in_=d_mask).then_inc(s_kv.h, 16)
                    s_kv.v += 16 * (Hc + 1)
                    for h in range(Hc):
                        for u in range(NU):
                            ui = Hc * NU + h * NU + u
                            ep = len(sq_h)
                            srcv = kT[:, h * T + u * cfg.QW: h * T + (u + 1) * cfg.QW]
                            if role == "sc":
                                eng.wait_ge(s_kv.h, 16 * (Hc + 1))
                                if ep >= 4:
                                    eng.wait_ge(s_nm.h, ep - 3)
                            sq_h.append(sop(role, eng, lambda e, srcv=srcv, ep=ep: e.activation(
                                sq_b[ep % 4][:], srcv, AF.Square)))
                            if role == "te":
                                eng.wait_ge(stk.h, sq_h[ep])
                                if ep >= 4:
                                    eng.wait_ge(vt.h, nr_h[ep - 4])
                                qt, qoff = _q4[ep % 4]
                                for k2 in range(cfg.QW // 512):
                                    mm = eng.matmul(qt[0:1, qoff + k2 * 512: qoff + (k2 + 1) * 512], ones_c[:],
                                                    sq_b[ep % 4][:, k2 * 512:(k2 + 1) * 512],
                                                    start=True, stop=True)
                                mm.then_inc(s_nm.h, 1)
                            s_nm.v += 1
                            if role == "ve":
                                eng.wait_ge(s_nm.h, s_nm.v)
                            qt, qoff = _q4[ep % 4]
                            nr_h.append(vop(role, eng, lambda e, ui=ui, qt=qt, qoff=qoff: e.tensor_reduce(
                                un_r[:, ui:ui + 1], qt[0:1, qoff:qoff + cfg.QW],
                                axis=mybir.AxisListType.X, op=ALU.max)))
                    s_qb = sem("d_qb")
                    for h in range(Hc):
                        if role == "sy":
                            if h == 0:
                                eng.wait_ge(s_cc.h, 2)
                            else:
                                eng.wait_ge(stk.h, sq_h[Hc * NU + h * NU - 1])
                            eng.dma_start(
                                out=v_s[:, 0:T].rearrange("p (r t) -> p r t", t=To),
                                in_=q3[h * 128:, :, :][bass.ds(dyn["gFW"], 128), :, :],
                            ).then_inc(s_qb.h, 16)
                        s_qb.v += 16
                        for u in range(NU):
                            ui = h * NU + u
                            ep = len(sq_h)
                            srcv = v_s[:, u * cfg.QW:(u + 1) * cfg.QW]
                            if role == "sc":
                                eng.wait_ge(s_qb.h, s_qb.v)
                                if ep >= 4:
                                    eng.wait_ge(s_nm.h, ep - 3)
                            sq_h.append(sop(role, eng, lambda e, srcv=srcv, ep=ep: e.activation(
                                sq_b[ep % 4][:], srcv, AF.Square)))
                            if role == "te":
                                eng.wait_ge(stk.h, sq_h[ep])
                                if ep >= 4:
                                    eng.wait_ge(vt.h, nr_h[ep - 4])
                                qt, qoff = _q4[ep % 4]
                                for k2 in range(cfg.QW // 512):
                                    mm = eng.matmul(qt[0:1, qoff + k2 * 512: qoff + (k2 + 1) * 512], ones_c[:],
                                                    sq_b[ep % 4][:, k2 * 512:(k2 + 1) * 512],
                                                    start=True, stop=True)
                                mm.then_inc(s_nm.h, 1)
                            s_nm.v += 1
                            if role == "ve":
                                eng.wait_ge(s_nm.h, s_nm.v)
                            qt, qoff = _q4[ep % 4]
                            nr_h.append(vop(role, eng, lambda e, ui=ui, qt=qt, qoff=qoff: e.tensor_reduce(
                                un_r[:, ui:ui + 1], qt[0:1, qoff:qoff + cfg.QW],
                                axis=mybir.AxisListType.X, op=ALU.max)))

                    for j in range(2 * Hc):
                        vop(role, eng, lambda e, j=j: e.tensor_reduce(
                            mx_r[:, j:j + 1], un_r[:, j * NU:(j + 1) * NU],
                            axis=mybir.AxisListType.X, op=ALU.max))
                    vop(role, eng, lambda e: e.tensor_tensor(s2_r[:], mx_r[:, 0:Hc], mx_r[:, Hc:2 * Hc], ALU.add))
                    mx_now = vop(role, eng, lambda e: e.tensor_scalar(
                        s2_r[:], s2_r[:], -0.5 * cfg.scale, None, ALU.mult, ALU.bypass))
                    if role == "te":
                        eng.wait_ge(vt.h, mx_now)
                        eng.matmul(psS[:, 0:Hc], ones_r[:], s2_r[:], start=True, stop=True).then_inc(s_nb.h, 1)
                    s_nb.v += 1
                    if role == "sc":
                        eng.wait_ge(s_nb.h, s_nb.v)
                    sop(role, eng, lambda e: e.copy(negM[:, 0:Hc], psS[:, 0:Hc]))
                    endp(role, eng, {"sy": [s_kv, sem("d_qb")], "ve": vt, "te": s_nb, "sc": stk})

                run_phase(ph4)

                if upto >= 5:
                    s_S = sem("e_S")
                    s_A = sem("e_A")
                    s_Zb = sem("e_Zb")
                    s_os = dpair("e_os")

                    sched = []
                    qcg0 = 0
                    for h in range(Hc):
                        for qc in range(cfg.n_qc):
                            sched.append((h, qc, cfg.ND * (qc + 1), qcg0))
                            qcg0 += 1
                    S_end = []
                    c1 = 0
                    for _, _, n_s, _ in sched:
                        c1 += n_s
                        S_end.append(c1)
                    NH = cfg.QW // 512

                    def ph5(role, eng):
                        q3b = q_gath.ap().rearrange("(r q) t -> q r t", q=D)
                        s_oc = sem("f_oc")
                        s_qf = sem("e_qf")
                        Zi_h, Ze_h, O_h, Eend_h = [], [], [], []
                        for h, qc, n_s, qcg in sched:
                            vt0 = vt.v
                            stk0 = stk.v
                            M_val = lambda j: vt0 + j + 1
                            E_val = lambda st: stk0 + st + 1
                            Zi_h.append(vt0 + cfg.ND + 1)
                            O_h.append(vt0 + cfg.ND + 2)
                            Ze_h.append(stk0 + n_s + 1)
                            Eend_h.append(stk0 + n_s)
                            prev_E_end = Eend_h[qcg - 1] if qcg >= 1 else 0
                            if qc == 0:
                                if role == "sy":
                                    if h >= 1:
                                        eng.wait_ge(s_S.h, S_end[h * cfg.n_qc - 1])
                                    eng.dma_start(
                                        out=q_f[:].rearrange("p (r t) -> p r t", t=To),
                                        in_=q3b[h * 128:, :, :][bass.ds(dyn["gFW"], 128), :, :],
                                    ).then_inc(s_qf.h, 16)
                                s_qf.v += 16
                                if h == 0:
                                    s_vl = sem("d_vl")
                                    if role == "sy":
                                        eng.wait_ge(s_cc.h, 3)
                                        eng.dma_start(
                                            out=v_s[:].rearrange("p (s f) -> p s f", f=cfg.FW),
                                            in_=v_gath.ap()[:, bass.ds(dyn["gFW"], cfg.FW)].rearrange(
                                                "(s p) f -> p s f", p=128),
                                        ).then_inc(s_vl.h, 16)
                                    s_vl.v += 16
                            ZV = psC[0:1, 0:cfg.QW]
                            OV = psB[:, 0:cfg.QW]
                            sS0 = s_S.v
                            sA0 = s_A.v

                            def av_z(pst, eng, h=h, n_s=n_s):
                                vv = v_s[:, pst * cfg.FW + h * 128: pst * cfg.FW + (h + 1) * 128]
                                pb = pT2[(pst // 2) % 2]
                                po = (pst % 2) * cfg.QW
                                st_, sp_ = (pst == 0), (pst == n_s - 1)
                                for k2 in range(NH):
                                    eng.matmul(OV[:, k2 * 512:(k2 + 1) * 512], vv,
                                               pb[:, po + k2 * 512: po + (k2 + 1) * 512], start=st_, stop=sp_)
                                for k2 in range(NH):
                                    mm = eng.matmul(ZV[:, k2 * 512:(k2 + 1) * 512], ones_c[:],
                                                    pb[:, po + k2 * 512: po + (k2 + 1) * 512], start=st_, stop=sp_)
                                mm.then_inc(s_A.h, 1)

                            if role == "te":
                                if qc == 0:
                                    eng.wait_ge(s_qf.h, s_qf.v)
                                if qcg >= 1:
                                    eng.wait_ge(vt.h, O_h[qcg - 1])
                                    eng.wait_ge(vt.h, Zi_h[qcg - 1])
                                for st in range(n_s):
                                    if st >= 2:
                                        eng.wait_ge(stk.h, E_val(st - 2))
                                    elif prev_E_end:
                                        eng.wait_ge(stk.h, prev_E_end)
                                    So = (st % 2) * cfg.QW
                                    kk = kT[:, h * T + st * 128: h * T + (st + 1) * 128]
                                    for k2 in range(NH):
                                        mm = eng.matmul(psS[:, So + k2 * 512: So + (k2 + 1) * 512], kk,
                                                        q_f[:, qc * cfg.QW + k2 * 512: qc * cfg.QW + (k2 + 1) * 512],
                                                        start=True, stop=True)
                                    mm.then_inc(s_S.h, 1)
                                    if st >= 1:
                                        pst = st - 1
                                        if qcg == 0 and st == 1:
                                            eng.wait_ge(sem("d_vl").h, 16)
                                        if pst >= n_s - cfg.ND:
                                            eng.wait_ge(vt.h, M_val(pst - (n_s - cfg.ND)))
                                        else:
                                            eng.wait_ge(stk.h, E_val(pst))
                                        av_z(pst, eng)
                                pst = n_s - 1
                                eng.wait_ge(vt.h, M_val(cfg.ND - 1))
                                av_z(pst, eng)
                            s_S.v += n_s
                            s_A.v += n_s
                            for st in range(n_s):
                                if role == "sc":
                                    eng.wait_ge(s_S.h, sS0 + st + 1)
                                    eng.wait_ge(s_A.h, sA0 + max(0, st - 3))
                                ev = sop(role, eng, lambda e, st=st, h=h: e.activation(
                                    pT2[(st // 2) % 2][:, (st % 2) * cfg.QW:(st % 2 + 1) * cfg.QW],
                                    psS[:, (st % 2) * cfg.QW:(st % 2 + 1) * cfg.QW], AF.Exp,
                                    scale=cfg.scale, bias=negM[:, h:h + 1]), relax=(st > 0))
                                assert ev == E_val(st), (ev, E_val(st))
                            for st in range(n_s - cfg.ND, n_s):
                                j = st - (n_s - cfg.ND)
                                off = (cfg.ND - 1) * 128 - 128 * j
                                pbm = pT2[(st // 2) % 2]
                                pom = (st % 2) * cfg.QW
                                if role == "ve":
                                    eng.wait_ge(stk.h, E_val(st))
                                mv = vop(role, eng, lambda e, pbm=pbm, pom=pom, off=off: e.tensor_tensor(
                                    pbm[:, pom:pom + cfg.QW], pbm[:, pom:pom + cfg.QW],
                                    mask_s[:, off: off + cfg.QW], ALU.mult), relax=(st > n_s - cfg.ND))
                                assert mv == M_val(j), (mv, M_val(j))
                            if role == "sc":
                                eng.wait_ge(s_A.h, s_A.v)
                                if qcg >= 1:
                                    eng.wait_ge(s_Zb.h, qcg)
                            ze_now = sop(role, eng, lambda e: e.copy(zinv[:], ZV))
                            assert ze_now == Ze_h[qcg]
                            if role == "te":
                                eng.wait_ge(stk.h, Ze_h[qcg])
                                if qcg >= 1:
                                    eng.wait_ge(vt.h, Zi_h[qcg - 1])
                                for k2 in range(NH):
                                    mm = eng.matmul(psS[:, k2 * 512:(k2 + 1) * 512], ones_r[:],
                                                    zinv[:, k2 * 512:(k2 + 1) * 512], start=True, stop=True)
                                mm.then_inc(s_Zb.h, 1)
                            s_Zb.v += 1
                            if role == "ve":
                                eng.wait_ge(s_Zb.h, s_Zb.v)
                            zv_now = vop(role, eng, lambda e: e.reciprocal(zb_sb[:], psS[:, 0:cfg.QW]))
                            assert zv_now == Zi_h[qcg]
                            if role == "ve":
                                s_os.wait(eng, qcg - 2)
                            o_now = vop(role, eng, lambda e, qcg=qcg: e.tensor_tensor(
                                o_b[qcg % 2][:], OV, zb_sb[:], ALU.mult))
                            assert o_now == O_h[qcg]
                            if role == "sy":
                                eng.wait_ge(vt.h, O_h[qcg])
                                s_os.add(eng.dma_start(
                                    out=o_contrib.ap()[h * 128:(h + 1) * 128, qc * cfg.QW:(qc + 1) * cfg.QW],
                                    in_=o_b[qcg % 2][:]), qcg)
                            else:
                                s_os.add(None, qcg)
                            if qc == cfg.n_qc - 1:
                                if role == "gp":
                                    for sp in s_os.sems():
                                        eng.wait_ge(sp.h, sp.v)
                                    eng.collective_compute(
                                        "AllGather", ALU.bypass, replica_groups=GROUPS,
                                        ins=[o_contrib.ap()[h * 128:(h + 1) * 128, :].opt()],
                                        outs=[o_gath.ap()[h * 4 * 128:(h + 1) * 4 * 128, :].opt()]).then_inc(s_oc.h)
                                s_oc.v += 1
                        endp(role, eng, {"sy": [sem("e_qf"), sem("d_vl"), s_os], "gp": s_oc, "ve": vt, "te": s_Zb, "sc": stk})

                    run_phase(ph5)

        if upto >= 6:
            with ExitStack() as stC:
                sbC = lambda n, s, d: stC.enter_context(nc.sbuf_tensor(n, s, d))
                avT = sbC("avT", [128, NCS * To], BF16)
                wo_s = sbC("wo_s", [128, NCS * D], BF16)
                bo_bc = sbC("bo_bc", [128, D], F32)
                y_st = sbC("y_st", [128, D], F32)
                ln_t = sbC("ln_t", [128, D], F32)
                x_rb = [sbC(f"x_rb{i}", [128, D], F32) for i in range(2)]
                x2_b = [sbC(f"x2_b{i}", [128, D], F32) for i in range(2)]
                xl3 = sbC("xl3", [128, D], F32)
                stg0 = sbC("stg0", [128, D], BF16)
                stg = [stg0, stg0]
                bn6b = sbC("bn6b", [128, 6 * NK], F32)
                mv2b = sbC("mv2b", [128, 2], F32)
                lnvb = sbC("lnvb", [128, 1], F32)
                rstdb = sbC("rstdb", [128, 1], F32)

                s_al = sem("f_al")
                s_wo2 = sem("f_wo")
                s_mo = sem("f_mo")
                s_tp2 = sem("f_tp2")
                s_xr = dpair("f_xr")
                s_x2s = dpair("f_x2s")
                s_3s = dpair("f_3s")

                def ph6(role, eng):
                    s_oc2 = sem("f_oc")
                    o4 = o_gath.ap().rearrange("(f p) t -> p f t", p=128)
                    for h in range(Hc):
                        if role == "sy":
                            eng.wait_ge(s_oc2.h, h + 1)
                            eng.dma_start(
                                out=avT[:, h * 4 * To:(h + 1) * 4 * To].rearrange("p (f t) -> p f t", t=To),
                                in_=o4[:, h * 4:(h + 1) * 4, bass.ds(dyn["gTo"], To)],
                            ).then_inc(s_al.h, 16)
                        s_al.v += 16
                    if role == "gp":
                        for fs in range(NCS):
                            eng.dma_start(out=wo_s[:, fs * D:(fs + 1) * D], in_=d_wo[fs, :, :]).then_inc(s_wo2.h, 16)
                    s_wo2.v += 16 * NCS
                    bcast_row(role, eng, d_bo, bo_bc, "f_b")
                    endp(role, eng, {"gp": s_wo2, "sc": stk})

                run_phase(ph6)

                if upto >= 7:
                    def ph7(role, eng):
                        oe_h, l2_h, l3_h, te2_h = [], [], [], []
                        for s in range(cfg.n_strip):
                            if role == "sy":
                                if s >= 2:
                                    eng.wait_ge(vt.h, l2_h[s - 2])
                                s_xr.add(eng.dma_start(out=x_rb[s % 2][:], in_=d_x[s * 128:(s + 1) * 128, :]), s)
                            else:
                                s_xr.add(None, s)
                            for oh in range(NK):
                                idx = s * NK + oh
                                if role == "te":
                                    if idx >= 4:
                                        eng.wait_ge(vt.h, oe_h[idx - 4])
                                    if s >= 2 and oh == 0:
                                        eng.wait_ge(s_al.h, 16 * Hc)
                                    for fs in range(NCS):
                                        if s < 2 and fs % 4 == 0:
                                            eng.wait_ge(s_al.h, 16 * (fs // 4 + 1))
                                        gfs = (fs % 4) * Hc + fs // 4
                                        mm = eng.matmul(slot(idx % 4)[:, :],
                                                        avT[:, fs * To + s * 128: fs * To + (s + 1) * 128],
                                                        wo_s[:, gfs * D + oh * 512: gfs * D + (oh + 1) * 512],
                                                        start=(fs == 0), stop=(fs == NCS - 1))
                                    mm.then_inc(s_mo.h, 1)
                                s_mo.v += 1
                                if role == "ve":
                                    eng.wait_ge(s_mo.h, s_mo.v)
                                oe_h.append(vop(role, eng, lambda e, idx=idx, oh=oh: e.scalar_tensor_tensor(
                                    y_st[:, oh * 512:(oh + 1) * 512], slot(idx % 4)[:, :],
                                    1.0, bo_bc[:, oh * 512:(oh + 1) * 512], ALU.mult, ALU.add)))
                            for k in range(NK):
                                vop(role, eng, lambda e, k=k: e.bn_stats(bn6b[:, 6 * k:6 * (k + 1)],
                                                                         y_st[:, 512 * k:512 * (k + 1)]))
                            bn_now = vop(role, eng, lambda e: e.bn_aggr(mv2b[:], bn6b[:]))
                            if role == "sc":
                                eng.wait_ge(vt.h, bn_now)
                            sop(role, eng, lambda e: e.activation(lnvb[:], mv2b[:, 1:2], AF.Ln, bias=eps_c[:, 0:1]))
                            rs_now = sop(role, eng, lambda e: e.activation(rstdb[:], lnvb[:], AF.Exp, scale=-0.5))
                            if role == "ve":
                                eng.wait_ge(stk.h, rs_now)
                            vop(role, eng, lambda e: e.tensor_scalar(ln_t[:], y_st[:], mv2b[:, 0:1], rstdb[:, 0:1],
                                                                     ALU.subtract, ALU.mult))
                            if role == "ve":
                                s_xr.wait(eng, s)
                                s_x2s.wait(eng, s - 2)
                            l2_h.append(vop(role, eng, lambda e, s=s: e.tensor_tensor(
                                x2_b[s % 2][:], ln_t[:], x_rb[s % 2][:], ALU.add)))
                            if role == "sy":
                                eng.wait_ge(vt.h, l2_h[s])
                                s_x2s.add(eng.dma_start(out=x2_sp.ap()[s * 128:(s + 1) * 128, :],
                                                        in_=x2_b[s % 2][:]), s)
                            else:
                                s_x2s.add(None, s)
                            for k in range(NK):
                                vop(role, eng, lambda e, k=k, s=s: e.bn_stats(
                                    bn6b[:, 6 * k:6 * (k + 1)], x2_b[s % 2][:, 512 * k:512 * (k + 1)]))
                            bn3_now = vop(role, eng, lambda e: e.bn_aggr(mv2b[:], bn6b[:]))
                            if role == "sc":
                                eng.wait_ge(vt.h, bn3_now)
                            sop(role, eng, lambda e: e.activation(lnvb[:], mv2b[:, 1:2], AF.Ln, bias=eps_c[:, 0:1]))
                            rs3_now = sop(role, eng, lambda e: e.activation(rstdb[:], lnvb[:], AF.Exp, scale=-0.5))
                            if role == "ve":
                                eng.wait_ge(stk.h, rs3_now)
                                if s >= 1:
                                    eng.wait_ge(s_tp2.h, s)
                            l3_h.append(vop(role, eng, lambda e, s=s: e.tensor_scalar(
                                xl3[:], x2_b[s % 2][:], mv2b[:, 0:1], rstdb[:, 0:1], ALU.subtract, ALU.mult)))
                            if role == "te":
                                eng.wait_ge(vt.h, l3_h[s])
                                if s >= 1:
                                    eng.wait_ge(stk.h, te2_h[s - 1])
                                for cs in range(NCS):
                                    p = psB if cs < 8 else psC
                                    mm = eng.transpose(p[:, (cs % 8) * 128:(cs % 8 + 1) * 128],
                                                       xl3[:, cs * 128:(cs + 1) * 128], id_s[:])
                                    if cs == NCS - 1:
                                        mm.then_inc(s_tp2.h, 1)
                            s_tp2.v += 1
                            if role == "sc":
                                eng.wait_ge(s_tp2.h, s_tp2.v)
                                s_3s.wait(eng, s - 1)
                            e_now = 0
                            for k in range(NK):
                                p = psB if k < 2 else psC
                                e_now = sop(role, eng, lambda e, k=k, s=s, p=p: e.copy(
                                    stg[s % 2][:, k * 512:(k + 1) * 512],
                                    p[:, (k % 2) * 512:(k % 2 + 1) * 512]))
                            te2_h.append(e_now)
                            if role == "sy":
                                eng.wait_ge(stk.h, te2_h[s])
                                s_3s.add(eng.dma_start(
                                    out=xln3T_sp.ap().rearrange("(c p) t -> p c t", p=128)[:, :, s * 128:(s + 1) * 128],
                                    in_=stg[s % 2][:].rearrange("p (c t) -> p c t", t=128),
                                ), s)
                            else:
                                s_3s.add(None, s)
                        endp(role, eng, {"sy": [s_xr, s_x2s, s_3s], "ve": vt, "te": s_tp2, "sc": stk})

                    run_phase(ph7)

        if upto >= 8:
            with ExitStack() as stD:
                sbD = lambda n, s, d: stD.enter_context(nc.sbuf_tensor(n, s, d))
                x3_b = [sbD(f"x3_b{i}", [128, NCS * 512], BF16) for i in range(2)]
                sd_bc = sbD("sd_bc", [128, D], F32)
                fuse = sbD("fuse", [128, cfg.n_ic * 512], BF16)
                wgu_b = [sbD(f"wgu_b{i}", [128, 2 * D], BF16) for i in range(3)]
                wd_b = [sbD(f"wd_b{i}", [128, 4 * 512], BF16) for i in range(4)]
                sil_b = [sbD(f"sil_b{i}", [128, 512], F32) for i in range(2)]
                ftok = sbD("ftok", [128, 4 * D], F32)
                x2r = [sbD(f"x2r{i}", [128, D], F32) for i in range(2)]
                bn6c = sbD("bn6c", [128, 6 * NK], F32)
                mv2c = sbD("mv2c", [128, 2], F32)
                lnvc = sbD("lnvc", [128, 1], F32)
                rstdc = sbD("rstdc", [128, 1], F32)

                s_f1g = sem("g_f1g")
                s_f1u = sem("g_f1u")
                s_f2m = sem("g_f2m")
                s_x3 = dpair("g_x3")
                s_wg = dpair("g_wg", n=3)
                s_wd = dpair("g_wd", n=4)
                s_x2r = dpair("g_x2r")
                s_ys = dpair("g_ys")

                def ph8(role, eng):
                    bc3_done = bcast_row(role, eng, d_sd, sd_bc, "g_b")
                    sil_h, fu_h, f2e_h, l4_h = [], [], [], []
                    def x3_load(eng, ttl):
                        return eng.dma_start(
                            out=x3_b[ttl % 2][:].rearrange("p (c t) -> p c t", t=512),
                            in_=xln3T_sp.ap().rearrange("(c p) t -> p c t", p=128)[:, :, ttl * 512:(ttl + 1) * 512],
                        )
                    for tt in range(cfg.n_tt):
                        if tt == 0:
                            s_x3.add(x3_load(eng, 0) if role == "sy" else None, 0)
                        if tt + 1 < cfg.n_tt:
                            if role == "sy":
                                if tt + 1 >= 2:
                                    eng.wait_ge(s_f1u.h, cfg.n_ic * tt)
                                s_x3.add(x3_load(eng, tt + 1), tt + 1)
                            else:
                                s_x3.add(None, tt + 1)
                        for ic in range(cfg.n_ic):
                            gic = tt * cfg.n_ic + ic
                            pre = (tt >= 1 and ic < 2)
                            if role == "gp":
                                if not pre:
                                    if gic >= 3:
                                        eng.wait_ge(s_f1u.h, gic - 2)
                                    s_wg.add(eng.dma_start(out=wgu_b[gic % 3][:], in_=d_wgu[ic, :, :],
                                                           max_dma_last_dim=2048), gic)
                            elif not pre:
                                s_wg.add(None, gic)
                            gsl = psS[:, (gic % 2) * 512:(gic % 2 + 1) * 512]
                            usl = psS[:, 1024 + (gic % 2) * 512: 1024 + (gic % 2 + 1) * 512]
                            if role == "te":
                                s_wg.wait(eng, gic)
                                if ic == 0:
                                    s_x3.wait(eng, tt)
                                    eng.wait_ge(stk.h, bc3_done)
                                    if tt >= 1:
                                        eng.wait_ge(vt.h, f2e_h[-1])
                                if gic >= 2:
                                    eng.wait_ge(stk.h, sil_h[gic - 2])
                                    eng.wait_ge(vt.h, fu_h[gic - 2])
                                for cs in range(NCS):
                                    mm = eng.matmul(gsl, wgu_b[gic % 3][:, cs * 128:(cs + 1) * 128],
                                                    x3_b[tt % 2][:, cs * 512:(cs + 1) * 512],
                                                    start=(cs == 0), stop=(cs == NCS - 1))
                                mm.then_inc(s_f1g.h, 1)
                                for cs in range(NCS):
                                    mm = eng.matmul(usl, wgu_b[gic % 3][:, D + cs * 128: D + (cs + 1) * 128],
                                                    x3_b[tt % 2][:, cs * 512:(cs + 1) * 512],
                                                    start=(cs == 0), stop=(cs == NCS - 1))
                                mm.then_inc(s_f1u.h, 1)
                            s_f1g.v += 1
                            s_f1u.v += 1
                            if role == "sc":
                                eng.wait_ge(s_f1g.h, s_f1g.v)
                                if gic >= 2:
                                    eng.wait_ge(vt.h, fu_h[gic - 2])
                            sil_h.append(sop(role, eng, lambda e, gic=gic, ic=ic, gsl=gsl: e.activation(
                                sil_b[gic % 2][:], gsl, AF.Silu, scale=sg_s[:, ic:ic + 1])))
                            if role == "ve":
                                eng.wait_ge(stk.h, sil_h[gic])
                                eng.wait_ge(s_f1u.h, s_f1u.v)
                            fu_h.append(vop(role, eng, lambda e, ic=ic, gic=gic, usl=usl: e.scalar_tensor_tensor(
                                fuse[:, ic * 512:(ic + 1) * 512], usl, su_s[:, ic:ic + 1],
                                sil_b[gic % 2][:], ALU.mult, ALU.mult)))
                        if tt + 1 < cfg.n_tt:
                            if role == "gp":
                                eng.wait_ge(s_f1u.h, cfg.n_ic * (tt + 1))
                                for icn in range(2):
                                    s_wg.add(eng.dma_start(out=wgu_b[((tt + 1) * cfg.n_ic + icn) % 3][:], in_=d_wgu[icn, :, :],
                                                           max_dma_last_dim=2048), (tt + 1) * cfg.n_ic + icn)
                            else:
                                for icn in range(2):
                                    s_wg.add(None, (tt + 1) * cfg.n_ic + icn)
                        dsl = [psB[:, 0:512], psB[:, 512:1024], psC[:, 0:512], psC[:, 512:1024]]
                        for oh in range(cfg.n_oh):
                            goh = tt * cfg.n_oh + oh
                            for grp in range(cfg.n_grp):
                                ggrp = goh * cfg.n_grp + grp
                                if role == "gp":
                                    if ggrp >= 4:
                                        eng.wait_ge(s_f2m.h, ggrp - 3)
                                    s_wd.add(eng.dma_start(
                                        out=wd_b[ggrp % 4][:].rearrange("p (n o) -> p n o", o=512),
                                        in_=d_wd[grp, :, :, oh * 512:(oh + 1) * 512],
                                    ), ggrp)
                                else:
                                    s_wd.add(None, ggrp)
                                if role == "te":
                                    s_wd.wait(eng, ggrp)
                                    if grp == 0:
                                        eng.wait_ge(vt.h, fu_h[-1])
                                        if goh >= 1:
                                            eng.wait_ge(vt.h, f2e_h[-1])
                                    for i4 in range(4):
                                        ic = grp * 4 + i4
                                        for t1 in range(4):
                                            mm = eng.matmul(dsl[t1],
                                                            fuse[:, ic * 512 + t1 * 128: ic * 512 + (t1 + 1) * 128],
                                                            wd_b[ggrp % 4][:, i4 * 512:(i4 + 1) * 512],
                                                            start=(grp == 0 and i4 == 0),
                                                            stop=(grp == cfg.n_grp - 1 and i4 == 3))
                                    mm.then_inc(s_f2m.h, 1)
                                s_f2m.v += 1
                            if role == "ve":
                                eng.wait_ge(s_f2m.h, s_f2m.v)
                                if oh == 0 and tt >= 1:
                                    for t1 in range(4):
                                        s_ys.wait(eng, (tt - 1) * 4 + t1)
                            e_now = 0
                            for t1 in range(4):
                                e_now = vop(role, eng, lambda e, t1=t1, oh=oh: e.scalar_tensor_tensor(
                                    ftok[:, t1 * D + oh * 512: t1 * D + (oh + 1) * 512],
                                    dsl[t1], 1.0, sd_bc[:, oh * 512:(oh + 1) * 512], ALU.mult, ALU.mult))
                            f2e_h.append(e_now)
                        for t1 in range(4):
                            gt = tt * 4 + t1
                            if role == "sy":
                                if gt >= 2:
                                    eng.wait_ge(vt.h, l4_h[gt - 2])
                                s_x2r.add(eng.dma_start(out=x2r[gt % 2][:],
                                                        in_=x2_sp.ap()[gt * 128:(gt + 1) * 128, :]), gt)
                            else:
                                s_x2r.add(None, gt)
                            for k in range(NK):
                                vop(role, eng, lambda e, k=k, t1=t1: e.bn_stats(
                                    bn6c[:, 6 * k:6 * (k + 1)], ftok[:, t1 * D + 512 * k: t1 * D + 512 * (k + 1)]))
                            bn4_now = vop(role, eng, lambda e: e.bn_aggr(mv2c[:], bn6c[:]))
                            if role == "sc":
                                eng.wait_ge(vt.h, bn4_now)
                            sop(role, eng, lambda e: e.activation(lnvc[:], mv2c[:, 1:2], AF.Ln, bias=eps_c[:, 0:1]))
                            rs4_now = sop(role, eng, lambda e: e.activation(rstdc[:], lnvc[:], AF.Exp, scale=-0.5))
                            if role == "ve":
                                eng.wait_ge(stk.h, rs4_now)
                            if role == "ve":
                                s_ys.wait(eng, gt - 2)
                            vop(role, eng, lambda e, t1=t1: e.tensor_scalar(
                                ftok[:, t1 * D:(t1 + 1) * D], ftok[:, t1 * D:(t1 + 1) * D],
                                mv2c[:, 0:1], rstdc[:, 0:1], ALU.subtract, ALU.mult))
                            if role == "ve":
                                s_x2r.wait(eng, gt)
                            l4_h.append(vop(role, eng, lambda e, t1=t1, gt=gt: e.tensor_tensor(
                                ftok[:, t1 * D:(t1 + 1) * D], ftok[:, t1 * D:(t1 + 1) * D],
                                x2r[gt % 2][:], ALU.add)))
                            if role == "sy":
                                eng.wait_ge(vt.h, l4_h[gt])
                                s_ys.add(eng.dma_start(out=d_y[gt * 128:(gt + 1) * 128, :],
                                                       in_=ftok[:, t1 * D:(t1 + 1) * D]), gt)
                            else:
                                s_ys.add(None, gt)
                    endp(role, eng, {"sy": [s_x3, s_x2r, s_ys], "gp": [s_wg, s_wd], "ve": vt,
                                     "te": s_f2m, "sc": stk})

                run_phase(ph8)

        if dbg_pairs:
            def phdbg(role, eng):
                s_d = sem("z_dbg")
                if role == "gp":
                    for dst, srct in dbg_pairs:
                        eng.dma_start(out=dst, in_=srct.ap()).then_inc(s_d.h, 16)
                s_d.v += 16 * len(dbg_pairs)
                if role == "gp":
                    eng.wait_ge(s_d.h, s_d.v)
                endp(role, eng, {})

            run_phase(phdbg)

    return nc



def _host_prep(cfg, inputs):
    D, To, T = cfg.D, cfg.To, cfg.T_seq
    w_qkv = np.asarray(inputs["L0_attn_w_qkv"])
    b_qkv = np.asarray(inputs["L0_attn_b_qkv"])
    w_out = np.asarray(inputs["L0_attn_w_out"])
    b_out = np.asarray(inputs["L0_attn_b_out"])
    w_gate = np.asarray(inputs["L0_ffn_w_gate"])
    w_up = np.asarray(inputs["L0_ffn_w_up"])
    w_down = np.asarray(inputs["L0_ffn_w_down"])
    s_gate = np.asarray(inputs["L0_ffn_s_gate"])
    s_up = np.asarray(inputs["L0_ffn_s_up"])
    s_down = np.asarray(inputs["L0_ffn_s_down"])

    heads = cfg.heads
    perm = np.concatenate([np.arange(0, 128, 2), np.arange(1, 128, 2)])
    rows = (np.arange(heads)[:, None] * 128 + perm[None, :]).ravel()
    wq = w_qkv[rows, :]
    wk = w_qkv[D + rows, :]
    wqk = np.concatenate([wq, wk], axis=0)
    d_wqk = np.ascontiguousarray(
        wqk.reshape(cfg.n_fc, 128, cfg.n_csub, 128).transpose(0, 3, 2, 1).reshape(cfg.n_fc, 128, D))
    bqk = np.concatenate([b_qkv[rows], b_qkv[D + rows]])
    d_bqk = np.ascontiguousarray(bqk.reshape(cfg.n_fc, 128).T)
    d_wv = np.ascontiguousarray(w_qkv[2 * D:3 * D, :].T.reshape(cfg.n_csub, 128, D))
    d_bv = np.ascontiguousarray(b_qkv[2 * D:3 * D].reshape(1, D))
    d_wo = np.ascontiguousarray(w_out.T.reshape(cfg.n_csub, 128, D))
    d_bo = np.ascontiguousarray(b_out.reshape(1, D))
    d_wg = w_gate.reshape(cfg.n_csub, 128, cfg.n_ic, 128).transpose(2, 1, 0, 3).reshape(cfg.n_ic, 128, D)
    d_wu = w_up.reshape(cfg.n_csub, 128, cfg.n_ic, 128).transpose(2, 1, 0, 3).reshape(cfg.n_ic, 128, D)
    d_wgu = np.ascontiguousarray(np.concatenate([d_wg, d_wu], axis=2))
    d_wd = np.ascontiguousarray(
        w_down.reshape(cfg.n_grp, 4, 128, D).transpose(0, 2, 1, 3))
    d_sg = np.ascontiguousarray(s_gate.reshape(cfg.n_ic, 128).T).astype(np.float32)
    d_su = np.ascontiguousarray(s_up.reshape(cfg.n_ic, 128).T).astype(np.float32)
    d_sd = np.ascontiguousarray(s_down.reshape(1, D)).astype(np.float32)

    j = np.arange(64, dtype=np.float64)
    theta = 10000.0 ** (-j / 64.0)
    mw = cfg.QW + (cfg.ND - 1) * 128
    si = np.arange(128)[:, None]
    v = np.arange(mw)[None, :]
    d_mask = (si <= (v - (cfg.ND - 1) * 128)).astype(np.float32).astype(BF)

    shared = dict(wqk=d_wqk, bqk=d_bqk, wv=d_wv, bv=d_bv, wo=d_wo, bo=d_bo,
                  wgu=d_wgu, wd=d_wd, sg=d_sg, su=d_su, sd=d_sd, maskP=d_mask)

    x_full = np.asarray(inputs["L0_attn_x"])
    in_maps = []
    for c in range(8):
        b, g = c // 4, c % 4
        pos = np.arange(g * To, (g + 1) * To, dtype=np.float64)
        ang = pos[None, :] * theta[:, None]
        cos64 = np.cos(ang)
        sin64 = np.sin(ang)
        cosP = np.concatenate([cos64, cos64], axis=0).astype(BF)
        sinP = np.concatenate([-sin64, sin64], axis=0).astype(BF)
        m = dict(shared)
        m["x"] = np.ascontiguousarray(x_full[b, g * To:(g + 1) * To, :])
        m["gix"] = np.array([[g]], dtype=np.int32)
        m["identF"] = np.eye(128, dtype=np.float32)
        m["cosP"] = np.ascontiguousarray(cosP)
        m["sinP"] = np.ascontiguousarray(sinP)
        in_maps.append(m)
    return in_maps


def kernel(**inputs):
    cfg = Cfg()
    in_maps = _host_prep(cfg, inputs)
    nc = build(cfg)
    res = run_bass_kernel_spmd(nc, in_maps, core_ids=list(range(8)))
    B, T, D = 2, cfg.T_seq, cfg.D
    y = np.empty((B, T, D), dtype=np.float32)
    To = cfg.To
    for c in range(8):
        b, g = c // 4, c % 4
        y[b, g * To:(g + 1) * To, :] = np.asarray(res.results[c]["y"], dtype=np.float32)
    return y


# version 19
# speedup vs baseline: 1.0060x; 1.0060x over previous
import numpy as np
import ml_dtypes
import concourse.bass as bass
import concourse.mybir as mybir
from concourse.bass_utils import run_bass_kernel_spmd
from contextlib import ExitStack

F32 = mybir.dt.float32
BF16 = mybir.dt.bfloat16
I8 = mybir.dt.int8
I32 = mybir.dt.int32
AF = mybir.ActivationFunctionType
ALU = mybir.AluOpType
BF = ml_dtypes.bfloat16

ROLES = ("sy", "gp", "ve", "te", "sc")
GROUPS = [[0, 1, 2, 3], [4, 5, 6, 7]]


class Cfg:
    def __init__(self, T_seq=8192, D=2048, INTER=8192):
        self.T_seq = T_seq
        self.To = T_seq // 4
        self.D = D
        self.INTER = INTER
        self.heads = D // 128
        self.Hc = self.heads // 4
        self.n_csub = D // 128
        self.n_fc = 2 * self.heads
        self.n_strip = self.To // 128
        self.QW = min(1024, self.To)
        self.ND = self.QW // 128
        self.FW = self.Hc * 128
        self.n_qc = T_seq // self.QW
        self.n_fh = D // 512
        self.n_tc = self.To // 128
        self.n_tt = self.To // 512
        self.n_ic = INTER // 128
        self.n_oh = D // 512
        self.n_grp = INTER // 512
        self.scale = 1.0 / float(np.sqrt(128.0))
        self.eps = 1e-5


class Sem:
    __slots__ = ("h", "v", "name")

    def __init__(self, h, name):
        self.h = h
        self.v = 0
        self.name = name


class DPair:

    def __init__(self, slist, per=16):
        self.s = tuple(slist)
        self.n = len(self.s)
        self.per = per

    def add(self, instr, i):
        s = self.s[i % self.n]
        s.v += self.per
        if instr is not None:
            instr.then_inc(s.h, self.per)

    def wait(self, eng, i):
        if i < 0:
            return
        s = self.s[i % self.n]
        eng.wait_ge(s.h, self.per * (i // self.n + 1))

    def sems(self):
        return list(self.s)


def build(cfg, dbg=0, upto=99):
    nc = bass.Bass("TRN2", target_bir_lowering=False)
    T, D, INTER, To, Hc = cfg.T_seq, cfg.D, cfg.INTER, cfg.To, cfg.Hc
    NCS = cfg.n_csub
    NK = D // 512

    d_x = nc.dram_tensor("x", [To, D], F32, kind="ExternalInput").ap()
    d_gix = nc.dram_tensor("gix", [1, 1], I32, kind="ExternalInput").ap()
    d_wqk = nc.dram_tensor("wqk", [cfg.n_fc, 128, D], F32, kind="ExternalInput").ap()
    d_bqk = nc.dram_tensor("bqk", [128, cfg.n_fc], F32, kind="ExternalInput").ap()
    d_wv = nc.dram_tensor("wv", [NCS, 128, D], F32, kind="ExternalInput").ap()
    d_bv = nc.dram_tensor("bv", [1, D], F32, kind="ExternalInput").ap()
    d_wo = nc.dram_tensor("wo", [NCS, 128, D], F32, kind="ExternalInput").ap()
    d_bo = nc.dram_tensor("bo", [1, D], F32, kind="ExternalInput").ap()
    d_wgu = nc.dram_tensor("wgu", [cfg.n_ic, 128, 2 * D], I8, kind="ExternalInput").ap()
    d_wd = nc.dram_tensor("wd", [cfg.n_grp, 128, 4, D], I8, kind="ExternalInput").ap()
    d_sg = nc.dram_tensor("sg", [128, cfg.n_ic], F32, kind="ExternalInput").ap()
    d_su = nc.dram_tensor("su", [128, cfg.n_ic], F32, kind="ExternalInput").ap()
    d_sd = nc.dram_tensor("sd", [1, D], F32, kind="ExternalInput").ap()
    d_cos = nc.dram_tensor("cosP", [128, To], BF16, kind="ExternalInput").ap()
    d_sin = nc.dram_tensor("sinP", [128, To], BF16, kind="ExternalInput").ap()
    d_mask = nc.dram_tensor("maskP", [128, cfg.QW + (cfg.ND - 1) * 128], BF16, kind="ExternalInput").ap()
    d_id = nc.dram_tensor("identF", [128, 128], F32, kind="ExternalInput").ap()
    d_y = nc.dram_tensor("y", [To, D], F32, kind="ExternalOutput").ap()

    qk_contrib = nc.dram_tensor("qk_contrib", [2 * D, To], BF16)
    k_gath = nc.dram_tensor("k_gath", [4 * D, To], BF16)
    q_gath = nc.dram_tensor("q_gath", [4 * D, To], BF16)
    v_contrib = nc.dram_tensor("v_contrib", [To, D], BF16)
    v_gath = nc.dram_tensor("v_gath", [T, D], BF16)
    o_contrib = nc.dram_tensor("o_contrib", [Hc * 128, T], BF16)
    o_gath = nc.dram_tensor("o_gath", [4 * Hc * 128, T], BF16)
    x2_sp = nc.dram_tensor("x2_sp", [To, D], F32)
    xln3T_sp = nc.dram_tensor("xln3T_sp", [D, To], BF16)

    dbg_pairs = []
    if dbg >= 1:
        for nm, srct in (("dbg_qkc", qk_contrib), ("dbg_vc", v_contrib)):
            dbg_pairs.append((nc.dram_tensor(nm, list(srct.shape), srct.dtype, kind="ExternalOutput").ap(), srct))
    if dbg >= 2:
        for nm, srct in (("dbg_oc", o_contrib), ("dbg_x2", x2_sp), ("dbg_x3", xln3T_sp)):
            dbg_pairs.append((nc.dram_tensor(nm, list(srct.shape), srct.dtype, kind="ExternalOutput").ap(), srct))

    with ExitStack() as st0:
        sb0 = lambda n, s, d: st0.enter_context(nc.sbuf_tensor(n, s, d))
        sems = {}

        def sem(name):
            if name not in sems:
                sems[name] = Sem(st0.enter_context(nc.semaphore(name)), name)
            return sems[name]

        def dpair(name, per=16, n=2):
            return DPair([sem(name + str(j)) for j in range(n)], per)

        psS = st0.enter_context(nc.psum_tensor("psS", [128, 2048], F32))
        psB = st0.enter_context(nc.psum_tensor("psB", [128, 1024], F32))
        psC = st0.enter_context(nc.psum_tensor("psC", [128, 1024], F32))
        _slots = [(psS, 0), (psS, 512), (psS, 1024), (psS, 1536),
                  (psB, 0), (psB, 512), (psC, 0), (psC, 512)]

        def slot(i):
            t, off = _slots[i % 8]
            return t[:, off:off + 512]

        _q4 = [(psS, 0), (psS, 1024), (psB, 0), (psC, 0)]

        ones_c = sb0("ones_c", [128, 1], BF16)
        ones_r = sb0("ones_r", [1, 128], F32)
        eps_c = sb0("eps_c", [128, 1], F32)
        gix_s = sb0("gix_s", [1, 1], I32)
        bqk_s = sb0("bqk_s", [128, cfg.n_fc], F32)
        sg_s = sb0("sg_s", [128, cfg.n_ic], F32)
        su_s = sb0("su_s", [128, cfg.n_ic], F32)
        negM = sb0("negM", [128, max(Hc, 1)], F32)
        row_s = sb0("row_s", [1, D], F32)
        id_s = sb0("id_s", [128, 128], F32)

        block = st0.enter_context(nc.Block())
        sbar = sem("sbar")
        vt = sem("vtick")
        stk = sem("stick")
        bar = {"v": 0}

        def vop(role, eng, fn, relax=False):
            if role == "ve":
                if vt.v and not relax:
                    eng.wait_ge(vt.h, vt.v)
                fn(eng).then_inc(vt.h, 1)
            vt.v += 1
            return vt.v

        def sop(role, eng, fn, relax=False):
            if role == "sc":
                if stk.v and not relax:
                    eng.wait_ge(stk.h, stk.v)
                fn(eng).then_inc(stk.h, 1)
            stk.v += 1
            return stk.v

        def run_phase(fn):
            snap = {k: s.v for k, s in sems.items()}
            finals = []
            for role in ROLES:
                for k, s in sems.items():
                    s.v = snap.get(k, 0)
                eng_deco = {"sy": block.sync, "gp": block.gpsimd, "ve": block.vector,
                            "te": block.tensor, "sc": block.scalar}[role]

                def section(e, _role=role):
                    e.wait_ge(sbar.h, bar["v"])
                    fn(_role, e)
                section.__name__ = f"{fn.__name__}_{role}"
                eng_deco(section)
                finals.append({k: s.v for k, s in sems.items()})
            for f in finals[1:]:
                bad = [k for k in finals[0] if finals[0][k] != f.get(k)]
                assert not bad, f"lockstep divergence in {fn.__name__}: {bad}"
            bar["v"] = sbar.v

        def endp(role, eng, last_map):
            last = last_map.get(role)
            if last is not None:
                if not isinstance(last, (list, tuple)):
                    last = [last]
                flat = []
                for l in last:
                    flat.extend(l.sems() if isinstance(l, DPair) else [l])
                for l in flat:
                    if l.v:
                        eng.wait_ge(l.h, l.v)
            eng.nop().then_inc(sbar.h, 1)
            sbar.v += 5

        def bcast_row(role, eng, drow, dst, pfx):
            s_ld, s_mm = sem(pfx + "ld"), sem(pfx + "mm")
            if role == "sy":
                eng.dma_start(out=row_s[:], in_=drow).then_inc(s_ld.h, 16)
            s_ld.v += 16
            for k in range(NK):
                if role == "te":
                    if k == 0:
                        eng.wait_ge(s_ld.h, s_ld.v)
                    eng.matmul(slot(k)[:, :], ones_r[:], row_s[:, k * 512:(k + 1) * 512],
                               start=True, stop=True).then_inc(s_mm.h, 1)
                s_mm.v += 1
                if role == "sc":
                    eng.wait_ge(s_mm.h, s_mm.v)
                sop(role, eng, lambda e, k=k: e.copy(dst[:, k * 512:(k + 1) * 512], slot(k)[:, :]))
            return stk.v

        dyn = {}

        def ph0(role, eng):
            s0 = sem("p0")
            vop(role, eng, lambda e: e.memset(ones_c[:], 1.0))
            vop(role, eng, lambda e: e.memset(ones_r[:], 1.0))
            vop(role, eng, lambda e: e.memset(eps_c[:], cfg.eps))
            if role == "sy":
                eng.dma_start(out=bqk_s[:], in_=d_bqk).then_inc(s0.h, 16)
                eng.dma_start(out=sg_s[:], in_=d_sg).then_inc(s0.h, 16)
                eng.dma_start(out=su_s[:], in_=d_su).then_inc(s0.h, 16)
                eng.dma_start(out=gix_s[:], in_=d_gix).then_inc(s0.h, 16)
                eng.dma_start(out=id_s[:], in_=d_id).then_inc(s0.h, 16)
            s0.v += 80
            if role == "sy":
                eng.wait_ge(s0.h, s0.v)
                rg = eng.alloc_register("rg")
                eng.reg_load(rg, gix_s[0:1, 0:1])
                gval = eng.snap(rg)
                dyn["gFW"] = gval * cfg.FW
                dyn["gTo"] = gval * To
            endp(role, eng, {"sy": s0, "ve": vt})

        run_phase(ph0)

        with ExitStack() as stA:
            sbA = lambda n, s, d: stA.enter_context(nc.sbuf_tensor(n, s, d))
            xln1T = sbA("xln1T", [128, NCS * To], BF16)
            bv_bc = sbA("bv_bc", [128, D], F32)

            with ExitStack() as stP:
                sbP = lambda n, s, d: stP.enter_context(nc.sbuf_tensor(n, s, d))
                x_st = [sbP(f"x_st{i}", [128, D], F32) for i in range(2)]
                ln_st = [sbP(f"ln_st{i}", [128, D], F32) for i in range(2)]
                bn6 = sbP("bn6", [128, 6 * NK], F32)
                mv2 = sbP("mv2", [128, 2], F32)
                lnv = sbP("lnv", [128, 1], F32)
                rstd = sbP("rstd", [128, 1], F32)
                s_x = dpair("a_x")
                s_tp = sem("a_tp")

                def ph1(role, eng):
                    ln_h, ev_h = [], []
                    for s in range(cfg.n_strip):
                        if role == "sy":
                            if s >= 2:
                                eng.wait_ge(vt.h, ln_h[s - 2])
                            s_x.add(eng.dma_start(out=x_st[s % 2][:], in_=d_x[s * 128:(s + 1) * 128, :]), s)
                        else:
                            s_x.add(None, s)
                        if role == "ve":
                            s_x.wait(eng, s)
                        for k in range(NK):
                            vop(role, eng, lambda e, k=k, s=s: e.bn_stats(
                                bn6[:, 6 * k:6 * (k + 1)], x_st[s % 2][:, 512 * k:512 * (k + 1)]))
                        bn_now = vop(role, eng, lambda e: e.bn_aggr(mv2[:], bn6[:]))
                        if role == "sc":
                            eng.wait_ge(vt.h, bn_now)
                        sop(role, eng, lambda e: e.activation(lnv[:], mv2[:, 1:2], AF.Ln, bias=eps_c[:, 0:1]))
                        rs_now = sop(role, eng, lambda e: e.activation(rstd[:], lnv[:], AF.Exp, scale=-0.5))
                        if role == "ve":
                            eng.wait_ge(stk.h, rs_now)
                            if s >= 2:
                                eng.wait_ge(s_tp.h, s - 1)
                        ln_h.append(vop(role, eng, lambda e, s=s: e.tensor_scalar(
                            ln_st[s % 2][:], x_st[s % 2][:], mv2[:, 0:1], rstd[:, 0:1],
                            ALU.subtract, ALU.mult)))
                        def tp_dst(s, cs):
                            if s % 2 == 0:
                                return psS[:, cs * 128:(cs + 1) * 128]
                            t2 = psB if cs < 8 else psC
                            return t2[:, (cs % 8) * 128:(cs % 8 + 1) * 128]
                        if role == "te":
                            eng.wait_ge(vt.h, ln_h[s])
                            if s >= 2:
                                eng.wait_ge(stk.h, ev_h[s - 2])
                            for cs in range(NCS):
                                mm = eng.transpose(tp_dst(s, cs),
                                                   ln_st[s % 2][:, cs * 128:(cs + 1) * 128], id_s[:])
                                if cs == NCS - 1:
                                    mm.then_inc(s_tp.h, 1)
                        s_tp.v += 1
                        if role == "sc":
                            eng.wait_ge(s_tp.h, s_tp.v)
                        e_now = 0
                        for k in range(NK):
                            if s % 2 == 0:
                                srcv = psS[:, k * 512:(k + 1) * 512]
                            else:
                                t2 = psB if k < 2 else psC
                                srcv = t2[:, (k % 2) * 512:(k % 2 + 1) * 512]
                            srcv = srcv.rearrange("p (c t) -> p c t", t=128)
                            dstv = xln1T[:].rearrange("p (c t) -> p c t", t=To)[:, 4 * k:4 * (k + 1),
                                                                                s * 128:(s + 1) * 128]
                            e_now = sop(role, eng, lambda e, srcv=srcv, dstv=dstv: e.copy(dstv, srcv))
                        ev_h.append(e_now)
                    endp(role, eng, {"sy": s_x, "ve": vt, "te": s_tp, "sc": stk})

                run_phase(ph1)

            if upto >= 2:
                with ExitStack() as stP:
                    sbP = lambda n, s, d: stP.enter_context(nc.sbuf_tensor(n, s, d))
                    cos_s = sbP("cos_s", [128, To], BF16)
                    sin_s = sbP("sin_s", [128, To], BF16)
                    wq_b = [sbP(f"wq_b{i}", [128, D], BF16) for i in range(2)]
                    qk_sb = [sbP(f"qk_sb{i}", [128, To], BF16) for i in range(2)]
                    rop_b = [sbP(f"rop_b{i}", [128, To], BF16) for i in range(2)]
                    swp = sbP("swp", [128, To], BF16)
                    tmp1 = sbP("tmp1", [128, To], BF16)
                    tmp2 = sbP("tmp2", [128, To], BF16)
                    s_cs = sem("b_cs")
                    s_mt = sem("b_mt")
                    s_wq = dpair("b_wq")
                    s_so = dpair("b_so")
                    n_th = To // 512

                    s_cc = sem("d_cc")

                    def ph2(role, eng):
                        qe_h, ro_h = [], []
                        if role == "sy":
                            eng.dma_start(out=cos_s[:], in_=d_cos).then_inc(s_cs.h, 16)
                            eng.dma_start(out=sin_s[:], in_=d_sin).then_inc(s_cs.h, 16)
                        s_cs.v += 32
                        fc_list = list(range(cfg.heads, cfg.n_fc)) + list(range(cfg.heads))
                        for i_p, fc in enumerate(fc_list):
                            if role == "gp":
                                if i_p >= 2:
                                    eng.wait_ge(s_mt.h, (i_p - 1) * n_th)
                                s_wq.add(eng.dma_start(out=wq_b[i_p % 2][:], in_=d_wqk[fc, :, :]), i_p)
                                if i_p == cfg.heads + 2:
                                    for sp in s_so.sems():
                                        eng.wait_ge(sp.h, 16 * (cfg.heads // 2))
                                    eng.collective_compute(
                                        "AllGather", ALU.bypass, replica_groups=GROUPS,
                                        ins=[qk_contrib.ap()[D:2 * D, :].opt()],
                                        outs=[k_gath.ap().opt()]).then_inc(s_cc.h)
                            else:
                                s_wq.add(None, i_p)
                            if i_p == cfg.heads + 2:
                                s_cc.v += 1
                            e_now = 0
                            for th in range(n_th):
                                i = i_p * n_th + th
                                if role == "te":
                                    if th == 0:
                                        s_wq.wait(eng, i_p)
                                        if i_p >= 2:
                                            eng.wait_ge(stk.h, qe_h[i_p - 2])
                                    for cs in range(NCS):
                                        mm = eng.matmul(slot(i)[:, :], wq_b[i_p % 2][:, cs * 128:(cs + 1) * 128],
                                                        xln1T[:, cs * To + th * 512: cs * To + (th + 1) * 512],
                                                        start=(cs == 0), stop=(cs == NCS - 1))
                                    mm.then_inc(s_mt.h, 1)
                                s_mt.v += 1
                                if role == "sc":
                                    eng.wait_ge(s_mt.h, s_mt.v)
                                    if th == 0 and i_p >= 2:
                                        eng.wait_ge(vt.h, ro_h[i_p - 2])
                                e_now = sop(role, eng, lambda e, i=i, fc=fc, i_p=i_p, th=th: e.activation(
                                    qk_sb[i_p % 2][:, th * 512:(th + 1) * 512], slot(i)[:, :],
                                    AF.Identity, bias=bqk_s[:, fc:fc + 1]))
                            qe_h.append(e_now)
                            if role == "ve":
                                eng.wait_ge(stk.h, qe_h[i_p])
                                eng.wait_ge(s_cs.h, 32)
                                s_so.wait(eng, i_p - 2)
                            a = qk_sb[i_p % 2]
                            vop(role, eng, lambda e, a=a: e.tensor_copy(swp[0:64, :], a[64:128, :]))
                            vop(role, eng, lambda e, a=a: e.tensor_copy(swp[64:128, :], a[0:64, :]))
                            vop(role, eng, lambda e, a=a: e.tensor_tensor(tmp1[:], a[:], cos_s[:], ALU.mult))
                            vop(role, eng, lambda e: e.tensor_tensor(tmp2[:], swp[:], sin_s[:], ALU.mult))
                            ro_h.append(vop(role, eng, lambda e, i_p=i_p: e.tensor_tensor(
                                rop_b[i_p % 2][:], tmp1[:], tmp2[:], ALU.add)))
                            if role == "sy":
                                eng.wait_ge(vt.h, ro_h[i_p])
                                s_so.add(eng.dma_start(out=qk_contrib.ap()[fc * 128:(fc + 1) * 128, :],
                                                       in_=rop_b[i_p % 2][:]), i_p)
                            else:
                                s_so.add(None, i_p)
                        endp(role, eng, {"sy": [s_cs, s_so], "gp": s_wq, "ve": vt, "te": s_mt, "sc": stk})

                    run_phase(ph2)

                with ExitStack() as stP:
                    sbP = lambda n, s, d: stP.enter_context(nc.sbuf_tensor(n, s, d))
                    wv_b = [sbP(f"wv_b{i}", [128, NCS * 512], BF16) for i in range(2)]
                    v_eb = [sbP(f"v_eb{i}", [128, 512], BF16) for i in range(2)]
                    s_mv = sem("c_mv")
                    s_wv = dpair("c_wv")
                    s_vso = dpair("c_vso")

                    def ph3(role, eng):
                        vev_h = []
                        if role == "gp":
                            for nmso in ("b_so0", "b_so1"):
                                spx = sem(nmso)
                                eng.wait_ge(spx.h, spx.v)
                            eng.collective_compute(
                                "AllGather", ALU.bypass, replica_groups=GROUPS,
                                ins=[qk_contrib.ap()[0:D, :].opt()],
                                outs=[q_gath.ap().opt()]).then_inc(s_cc.h)
                        s_cc.v += 1
                        bc_done = bcast_row(role, eng, d_bv, bv_bc, "c_b")
                        for fh in range(cfg.n_fh):
                            if role == "gp":
                                if fh >= 2:
                                    eng.wait_ge(s_mv.h, (fh - 1) * cfg.n_tc)
                                s_wv.add(eng.dma_start(
                                    out=wv_b[fh % 2][:].rearrange("p (c f) -> p c f", f=512),
                                    in_=d_wv[:, :, fh * 512:(fh + 1) * 512].rearrange("c p f -> p c f"),
                                ), fh)
                            else:
                                s_wv.add(None, fh)
                            for tc in range(cfg.n_tc):
                                i = fh * cfg.n_tc + tc
                                if role == "te":
                                    if tc == 0:
                                        s_wv.wait(eng, fh)
                                        eng.wait_ge(stk.h, bc_done)
                                    if i >= 7:
                                        eng.wait_ge(vt.h, vev_h[i - 7])
                                    for cs in range(NCS):
                                        mm = eng.matmul(slot(i)[:, :],
                                                        xln1T[:, cs * To + tc * 128: cs * To + (tc + 1) * 128],
                                                        wv_b[fh % 2][:, cs * 512:(cs + 1) * 512],
                                                        start=(cs == 0), stop=(cs == NCS - 1))
                                    mm.then_inc(s_mv.h, 1)
                                s_mv.v += 1
                                if role == "ve":
                                    eng.wait_ge(s_mv.h, s_mv.v)
                                    s_vso.wait(eng, i - 2)
                                vev_h.append(vop(role, eng, lambda e, i=i, fh=fh: e.scalar_tensor_tensor(
                                    v_eb[i % 2][:], slot(i)[:, :], 1.0,
                                    bv_bc[:, fh * 512:(fh + 1) * 512], ALU.mult, ALU.add)))
                                if role == "sy":
                                    eng.wait_ge(vt.h, vev_h[i])
                                    s_vso.add(eng.dma_start(
                                        out=v_contrib.ap()[tc * 128:(tc + 1) * 128, fh * 512:(fh + 1) * 512],
                                        in_=v_eb[i % 2][:]), i)
                                else:
                                    s_vso.add(None, i)
                        endp(role, eng, {"sy": s_vso, "gp": s_wv, "ve": vt, "te": s_mv, "sc": stk})

                    run_phase(ph3)

        if upto >= 4:
            with ExitStack() as stB:
                sbB = lambda n, s, d: stB.enter_context(nc.sbuf_tensor(n, s, d))
                kT = sbB("kT", [128, Hc * T], BF16)
                v_s = sbB("v_s", [128, (T // 128) * cfg.FW], BF16)
                mask_s = sbB("mask_s", [128, cfg.QW + (cfg.ND - 1) * 128], BF16)
                q_f = sbB("q_f", [128, T], BF16)
                pT2 = [sbB(f"pT2_{i}", [128, 2 * cfg.QW], BF16) for i in range(2)]
                o_b = [sbB(f"o_b{i}", [128, cfg.QW], BF16) for i in range(2)]
                zinv = sbB("zinv", [1, cfg.QW], F32)
                zb_sb = sbB("zb_sb", [128, cfg.QW], F32)
                sq_b = [sbB(f"sq_b{i}", [128, cfg.QW], BF16) for i in range(4)]
                un_r = sbB("un_r", [1, 2 * Hc * (T // cfg.QW)], F32)
                mx_r = sbB("mx_r", [1, 2 * Hc], F32)
                s2_r = sbB("s2_r", [1, Hc], F32)

                NU = T // cfg.QW
                s_cc = sem("d_cc")
                s_kv = sem("d_kv")
                s_nm = sem("d_nm")
                s_nb = sem("d_nb")

                def ph4(role, eng):
                    sq_h, nr_h = [], []
                    if role == "gp":
                        for nmso in ("c_vso0", "c_vso1"):
                            spx = sem(nmso)
                            eng.wait_ge(spx.h, spx.v)
                        eng.collective_compute("AllGather", ALU.bypass, replica_groups=GROUPS,
                                               ins=[v_contrib.ap().opt()], outs=[v_gath.ap().opt()]).then_inc(s_cc.h)
                    s_cc.v += 1
                    k3 = k_gath.ap().rearrange("(r q) t -> q r t", q=D)
                    q3 = q_gath.ap().rearrange("(r q) t -> q r t", q=D)
                    if role == "sy":
                        eng.wait_ge(s_cc.h, 1)
                        for h in range(Hc):
                            eng.dma_start(
                                out=kT[:, h * T:(h + 1) * T].rearrange("p (r t) -> p r t", t=To),
                                in_=k3[h * 128:, :, :][bass.ds(dyn["gFW"], 128), :, :],
                            ).then_inc(s_kv.h, 16)
                        eng.dma_start(out=mask_s[:], in_=d_mask).then_inc(s_kv.h, 16)
                    s_kv.v += 16 * (Hc + 1)
                    for h in range(Hc):
                        for u in range(NU):
                            ui = Hc * NU + h * NU + u
                            ep = len(sq_h)
                            srcv = kT[:, h * T + u * cfg.QW: h * T + (u + 1) * cfg.QW]
                            if role == "sc":
                                eng.wait_ge(s_kv.h, 16 * (Hc + 1))
                                if ep >= 4:
                                    eng.wait_ge(s_nm.h, ep - 3)
                            sq_h.append(sop(role, eng, lambda e, srcv=srcv, ep=ep: e.activation(
                                sq_b[ep % 4][:], srcv, AF.Square)))
                            if role == "te":
                                eng.wait_ge(stk.h, sq_h[ep])
                                if ep >= 4:
                                    eng.wait_ge(vt.h, nr_h[ep - 4])
                                qt, qoff = _q4[ep % 4]
                                for k2 in range(cfg.QW // 512):
                                    mm = eng.matmul(qt[0:1, qoff + k2 * 512: qoff + (k2 + 1) * 512], ones_c[:],
                                                    sq_b[ep % 4][:, k2 * 512:(k2 + 1) * 512],
                                                    start=True, stop=True)
                                mm.then_inc(s_nm.h, 1)
                            s_nm.v += 1
                            if role == "ve":
                                eng.wait_ge(s_nm.h, s_nm.v)
                            qt, qoff = _q4[ep % 4]
                            nr_h.append(vop(role, eng, lambda e, ui=ui, qt=qt, qoff=qoff: e.tensor_reduce(
                                un_r[:, ui:ui + 1], qt[0:1, qoff:qoff + cfg.QW],
                                axis=mybir.AxisListType.X, op=ALU.max)))
                    s_qb = sem("d_qb")
                    for h in range(Hc):
                        if role == "sy":
                            if h == 0:
                                eng.wait_ge(s_cc.h, 2)
                            elif h >= 2:
                                eng.wait_ge(stk.h, sq_h[Hc * NU + (h - 1) * NU - 1])
                            eng.dma_start(
                                out=v_s[:, (h % 2) * T:(h % 2) * T + T].rearrange("p (r t) -> p r t", t=To),
                                in_=q3[h * 128:, :, :][bass.ds(dyn["gFW"], 128), :, :],
                            ).then_inc(s_qb.h, 16)
                        s_qb.v += 16
                        for u in range(NU):
                            ui = h * NU + u
                            ep = len(sq_h)
                            srcv = v_s[:, (h % 2) * T + u * cfg.QW:(h % 2) * T + (u + 1) * cfg.QW]
                            if role == "sc":
                                eng.wait_ge(s_qb.h, s_qb.v)
                                if ep >= 4:
                                    eng.wait_ge(s_nm.h, ep - 3)
                            sq_h.append(sop(role, eng, lambda e, srcv=srcv, ep=ep: e.activation(
                                sq_b[ep % 4][:], srcv, AF.Square)))
                            if role == "te":
                                eng.wait_ge(stk.h, sq_h[ep])
                                if ep >= 4:
                                    eng.wait_ge(vt.h, nr_h[ep - 4])
                                qt, qoff = _q4[ep % 4]
                                for k2 in range(cfg.QW // 512):
                                    mm = eng.matmul(qt[0:1, qoff + k2 * 512: qoff + (k2 + 1) * 512], ones_c[:],
                                                    sq_b[ep % 4][:, k2 * 512:(k2 + 1) * 512],
                                                    start=True, stop=True)
                                mm.then_inc(s_nm.h, 1)
                            s_nm.v += 1
                            if role == "ve":
                                eng.wait_ge(s_nm.h, s_nm.v)
                            qt, qoff = _q4[ep % 4]
                            nr_h.append(vop(role, eng, lambda e, ui=ui, qt=qt, qoff=qoff: e.tensor_reduce(
                                un_r[:, ui:ui + 1], qt[0:1, qoff:qoff + cfg.QW],
                                axis=mybir.AxisListType.X, op=ALU.max)))

                    for j in range(2 * Hc):
                        vop(role, eng, lambda e, j=j: e.tensor_reduce(
                            mx_r[:, j:j + 1], un_r[:, j * NU:(j + 1) * NU],
                            axis=mybir.AxisListType.X, op=ALU.max))
                    vop(role, eng, lambda e: e.tensor_tensor(s2_r[:], mx_r[:, 0:Hc], mx_r[:, Hc:2 * Hc], ALU.add))
                    mx_now = vop(role, eng, lambda e: e.tensor_scalar(
                        s2_r[:], s2_r[:], -0.5 * cfg.scale, None, ALU.mult, ALU.bypass))
                    if role == "te":
                        eng.wait_ge(vt.h, mx_now)
                        eng.matmul(psS[:, 0:Hc], ones_r[:], s2_r[:], start=True, stop=True).then_inc(s_nb.h, 1)
                    s_nb.v += 1
                    if role == "sc":
                        eng.wait_ge(s_nb.h, s_nb.v)
                    sop(role, eng, lambda e: e.copy(negM[:, 0:Hc], psS[:, 0:Hc]))
                    endp(role, eng, {"sy": [s_kv, sem("d_qb")], "ve": vt, "te": s_nb, "sc": stk})

                run_phase(ph4)

                if upto >= 5:
                    s_S = sem("e_S")
                    s_A = sem("e_A")
                    s_Zb = sem("e_Zb")
                    s_os = dpair("e_os")

                    sched = []
                    qcg0 = 0
                    for h in range(Hc):
                        for qc in range(cfg.n_qc):
                            sched.append((h, qc, cfg.ND * (qc + 1), qcg0))
                            qcg0 += 1
                    S_end = []
                    c1 = 0
                    for _, _, n_s, _ in sched:
                        c1 += n_s
                        S_end.append(c1)
                    NH = cfg.QW // 512

                    def ph5(role, eng):
                        q3b = q_gath.ap().rearrange("(r q) t -> q r t", q=D)
                        s_oc = sem("f_oc")
                        s_qf = sem("e_qf")
                        Zi_h, Ze_h, O_h, Eend_h = [], [], [], []
                        for h, qc, n_s, qcg in sched:
                            vt0 = vt.v
                            stk0 = stk.v
                            M_val = lambda j: vt0 + j + 1
                            E_val = lambda st: stk0 + st + 1
                            Zi_h.append(vt0 + cfg.ND + 1)
                            O_h.append(vt0 + cfg.ND + 2)
                            Ze_h.append(stk0 + n_s + 1)
                            Eend_h.append(stk0 + n_s)
                            prev_E_end = Eend_h[qcg - 1] if qcg >= 1 else 0
                            if qc == 0:
                                if role == "sy":
                                    if h >= 1:
                                        eng.wait_ge(s_S.h, S_end[h * cfg.n_qc - 1])
                                    eng.dma_start(
                                        out=q_f[:].rearrange("p (r t) -> p r t", t=To),
                                        in_=q3b[h * 128:, :, :][bass.ds(dyn["gFW"], 128), :, :],
                                    ).then_inc(s_qf.h, 16)
                                s_qf.v += 16
                                if h == 0:
                                    s_vl = sem("d_vl")
                                    if role == "sy":
                                        eng.wait_ge(s_cc.h, 3)
                                        eng.dma_start(
                                            out=v_s[:].rearrange("p (s f) -> p s f", f=cfg.FW),
                                            in_=v_gath.ap()[:, bass.ds(dyn["gFW"], cfg.FW)].rearrange(
                                                "(s p) f -> p s f", p=128),
                                        ).then_inc(s_vl.h, 16)
                                    s_vl.v += 16
                            ZV = psC[0:1, 0:cfg.QW]
                            OV = psB[:, 0:cfg.QW]
                            sS0 = s_S.v
                            sA0 = s_A.v

                            def av_z(pst, eng, h=h, n_s=n_s):
                                vv = v_s[:, pst * cfg.FW + h * 128: pst * cfg.FW + (h + 1) * 128]
                                pb = pT2[(pst // 2) % 2]
                                po = (pst % 2) * cfg.QW
                                st_, sp_ = (pst == 0), (pst == n_s - 1)
                                for k2 in range(NH):
                                    eng.matmul(OV[:, k2 * 512:(k2 + 1) * 512], vv,
                                               pb[:, po + k2 * 512: po + (k2 + 1) * 512], start=st_, stop=sp_)
                                for k2 in range(NH):
                                    mm = eng.matmul(ZV[:, k2 * 512:(k2 + 1) * 512], ones_c[:],
                                                    pb[:, po + k2 * 512: po + (k2 + 1) * 512], start=st_, stop=sp_)
                                mm.then_inc(s_A.h, 1)

                            if role == "te":
                                if qc == 0:
                                    eng.wait_ge(s_qf.h, s_qf.v)
                                if qcg >= 1:
                                    eng.wait_ge(vt.h, O_h[qcg - 1])
                                    eng.wait_ge(vt.h, Zi_h[qcg - 1])
                                for st in range(n_s):
                                    if st >= 2:
                                        eng.wait_ge(stk.h, E_val(st - 2))
                                    elif prev_E_end:
                                        eng.wait_ge(stk.h, prev_E_end)
                                    So = (st % 2) * cfg.QW
                                    kk = kT[:, h * T + st * 128: h * T + (st + 1) * 128]
                                    for k2 in range(NH):
                                        mm = eng.matmul(psS[:, So + k2 * 512: So + (k2 + 1) * 512], kk,
                                                        q_f[:, qc * cfg.QW + k2 * 512: qc * cfg.QW + (k2 + 1) * 512],
                                                        start=True, stop=True)
                                    mm.then_inc(s_S.h, 1)
                                    if st >= 1:
                                        pst = st - 1
                                        if qcg == 0 and st == 1:
                                            eng.wait_ge(sem("d_vl").h, 16)
                                        if pst >= n_s - cfg.ND:
                                            eng.wait_ge(vt.h, M_val(pst - (n_s - cfg.ND)))
                                        else:
                                            eng.wait_ge(stk.h, E_val(pst))
                                        av_z(pst, eng)
                                pst = n_s - 1
                                eng.wait_ge(vt.h, M_val(cfg.ND - 1))
                                av_z(pst, eng)
                            s_S.v += n_s
                            s_A.v += n_s
                            for st in range(n_s):
                                if role == "sc":
                                    eng.wait_ge(s_S.h, sS0 + st + 1)
                                    eng.wait_ge(s_A.h, sA0 + max(0, st - 3))
                                ev = sop(role, eng, lambda e, st=st, h=h: e.activation(
                                    pT2[(st // 2) % 2][:, (st % 2) * cfg.QW:(st % 2 + 1) * cfg.QW],
                                    psS[:, (st % 2) * cfg.QW:(st % 2 + 1) * cfg.QW], AF.Exp,
                                    scale=cfg.scale, bias=negM[:, h:h + 1]), relax=(st > 0))
                                assert ev == E_val(st), (ev, E_val(st))
                            for st in range(n_s - cfg.ND, n_s):
                                j = st - (n_s - cfg.ND)
                                off = (cfg.ND - 1) * 128 - 128 * j
                                pbm = pT2[(st // 2) % 2]
                                pom = (st % 2) * cfg.QW
                                if role == "ve":
                                    eng.wait_ge(stk.h, E_val(st))
                                mv = vop(role, eng, lambda e, pbm=pbm, pom=pom, off=off: e.tensor_tensor(
                                    pbm[:, pom:pom + cfg.QW], pbm[:, pom:pom + cfg.QW],
                                    mask_s[:, off: off + cfg.QW], ALU.mult), relax=(st > n_s - cfg.ND))
                                assert mv == M_val(j), (mv, M_val(j))
                            if role == "sc":
                                eng.wait_ge(s_A.h, s_A.v)
                                if qcg >= 1:
                                    eng.wait_ge(s_Zb.h, qcg)
                            ze_now = sop(role, eng, lambda e: e.copy(zinv[:], ZV))
                            assert ze_now == Ze_h[qcg]
                            if role == "te":
                                eng.wait_ge(stk.h, Ze_h[qcg])
                                if qcg >= 1:
                                    eng.wait_ge(vt.h, Zi_h[qcg - 1])
                                for k2 in range(NH):
                                    mm = eng.matmul(psS[:, k2 * 512:(k2 + 1) * 512], ones_r[:],
                                                    zinv[:, k2 * 512:(k2 + 1) * 512], start=True, stop=True)
                                mm.then_inc(s_Zb.h, 1)
                            s_Zb.v += 1
                            if role == "ve":
                                eng.wait_ge(s_Zb.h, s_Zb.v)
                            zv_now = vop(role, eng, lambda e: e.reciprocal(zb_sb[:], psS[:, 0:cfg.QW]))
                            assert zv_now == Zi_h[qcg]
                            if role == "ve":
                                s_os.wait(eng, qcg - 2)
                            o_now = vop(role, eng, lambda e, qcg=qcg: e.tensor_tensor(
                                o_b[qcg % 2][:], OV, zb_sb[:], ALU.mult))
                            assert o_now == O_h[qcg]
                            if role == "sy":
                                eng.wait_ge(vt.h, O_h[qcg])
                                s_os.add(eng.dma_start(
                                    out=o_contrib.ap()[h * 128:(h + 1) * 128, qc * cfg.QW:(qc + 1) * cfg.QW],
                                    in_=o_b[qcg % 2][:]), qcg)
                            else:
                                s_os.add(None, qcg)
                            if qc == cfg.n_qc - 1:
                                if role == "gp":
                                    for sp in s_os.sems():
                                        eng.wait_ge(sp.h, sp.v)
                                    eng.collective_compute(
                                        "AllGather", ALU.bypass, replica_groups=GROUPS,
                                        ins=[o_contrib.ap()[h * 128:(h + 1) * 128, :].opt()],
                                        outs=[o_gath.ap()[h * 4 * 128:(h + 1) * 4 * 128, :].opt()]).then_inc(s_oc.h)
                                s_oc.v += 1
                        endp(role, eng, {"sy": [sem("e_qf"), sem("d_vl"), s_os], "gp": s_oc, "ve": vt, "te": s_Zb, "sc": stk})

                    run_phase(ph5)

        if upto >= 6:
            with ExitStack() as stC:
                sbC = lambda n, s, d: stC.enter_context(nc.sbuf_tensor(n, s, d))
                avT = sbC("avT", [128, NCS * To], BF16)
                wo_s = sbC("wo_s", [128, NCS * D], BF16)
                bo_bc = sbC("bo_bc", [128, D], F32)
                y_st = sbC("y_st", [128, D], F32)
                ln_t = sbC("ln_t", [128, D], F32)
                x_rb = [sbC(f"x_rb{i}", [128, D], F32) for i in range(2)]
                x2_b = [sbC(f"x2_b{i}", [128, D], F32) for i in range(2)]
                xl3 = sbC("xl3", [128, D], F32)
                stg0 = sbC("stg0", [128, D], BF16)
                stg = [stg0, stg0]
                bn6b = sbC("bn6b", [128, 6 * NK], F32)
                mv2b = sbC("mv2b", [128, 2], F32)
                lnvb = sbC("lnvb", [128, 1], F32)
                rstdb = sbC("rstdb", [128, 1], F32)

                s_al = sem("f_al")
                s_wo2 = sem("f_wo")
                s_mo = sem("f_mo")
                s_tp2 = sem("f_tp2")
                s_xr = dpair("f_xr")
                s_x2s = dpair("f_x2s")
                s_3s = dpair("f_3s")

                def ph6(role, eng):
                    s_oc2 = sem("f_oc")
                    bcast_row(role, eng, d_bo, bo_bc, "f_b")
                    o4 = o_gath.ap().rearrange("(f p) t -> p f t", p=128)
                    for h in range(Hc):
                        if role == "sy":
                            eng.wait_ge(s_oc2.h, h + 1)
                            eng.dma_start(
                                out=avT[:, h * 4 * To:(h + 1) * 4 * To].rearrange("p (f t) -> p f t", t=To),
                                in_=o4[:, h * 4:(h + 1) * 4, bass.ds(dyn["gTo"], To)],
                            ).then_inc(s_al.h, 16)
                        s_al.v += 16
                    if role == "gp":
                        for fs in range(NCS):
                            eng.dma_start(out=wo_s[:, fs * D:(fs + 1) * D], in_=d_wo[fs, :, :]).then_inc(s_wo2.h, 16)
                    s_wo2.v += 16 * NCS
                    endp(role, eng, {"gp": s_wo2, "sc": stk})

                run_phase(ph6)

                if upto >= 7:
                    def ph7(role, eng):
                        oe_h, l2_h, l3_h, te2_h = [], [], [], []
                        for s in range(cfg.n_strip):
                            if role == "sy":
                                if s >= 2:
                                    eng.wait_ge(vt.h, l2_h[s - 2])
                                s_xr.add(eng.dma_start(out=x_rb[s % 2][:], in_=d_x[s * 128:(s + 1) * 128, :]), s)
                            else:
                                s_xr.add(None, s)
                            for oh in range(NK):
                                idx = s * NK + oh
                                if role == "te":
                                    if idx >= 4:
                                        eng.wait_ge(vt.h, oe_h[idx - 4])
                                    if s >= 2 and oh == 0:
                                        eng.wait_ge(s_al.h, 16 * Hc)
                                    for fs in range(NCS):
                                        if s < 2 and fs % 4 == 0:
                                            eng.wait_ge(s_al.h, 16 * (fs // 4 + 1))
                                        gfs = (fs % 4) * Hc + fs // 4
                                        mm = eng.matmul(slot(idx % 4)[:, :],
                                                        avT[:, fs * To + s * 128: fs * To + (s + 1) * 128],
                                                        wo_s[:, gfs * D + oh * 512: gfs * D + (oh + 1) * 512],
                                                        start=(fs == 0), stop=(fs == NCS - 1))
                                    mm.then_inc(s_mo.h, 1)
                                s_mo.v += 1
                                if role == "ve":
                                    eng.wait_ge(s_mo.h, s_mo.v)
                                oe_h.append(vop(role, eng, lambda e, idx=idx, oh=oh: e.scalar_tensor_tensor(
                                    y_st[:, oh * 512:(oh + 1) * 512], slot(idx % 4)[:, :],
                                    1.0, bo_bc[:, oh * 512:(oh + 1) * 512], ALU.mult, ALU.add)))
                            for k in range(NK):
                                vop(role, eng, lambda e, k=k: e.bn_stats(bn6b[:, 6 * k:6 * (k + 1)],
                                                                         y_st[:, 512 * k:512 * (k + 1)]))
                            bn_now = vop(role, eng, lambda e: e.bn_aggr(mv2b[:], bn6b[:]))
                            if role == "sc":
                                eng.wait_ge(vt.h, bn_now)
                            sop(role, eng, lambda e: e.activation(lnvb[:], mv2b[:, 1:2], AF.Ln, bias=eps_c[:, 0:1]))
                            rs_now = sop(role, eng, lambda e: e.activation(rstdb[:], lnvb[:], AF.Exp, scale=-0.5))
                            if role == "ve":
                                eng.wait_ge(stk.h, rs_now)
                            vop(role, eng, lambda e: e.tensor_scalar(ln_t[:], y_st[:], mv2b[:, 0:1], rstdb[:, 0:1],
                                                                     ALU.subtract, ALU.mult))
                            if role == "ve":
                                s_xr.wait(eng, s)
                                s_x2s.wait(eng, s - 2)
                            l2_h.append(vop(role, eng, lambda e, s=s: e.tensor_tensor(
                                x2_b[s % 2][:], ln_t[:], x_rb[s % 2][:], ALU.add)))
                            if role == "sy":
                                eng.wait_ge(vt.h, l2_h[s])
                                s_x2s.add(eng.dma_start(out=x2_sp.ap()[s * 128:(s + 1) * 128, :],
                                                        in_=x2_b[s % 2][:]), s)
                            else:
                                s_x2s.add(None, s)
                            for k in range(NK):
                                vop(role, eng, lambda e, k=k, s=s: e.bn_stats(
                                    bn6b[:, 6 * k:6 * (k + 1)], x2_b[s % 2][:, 512 * k:512 * (k + 1)]))
                            bn3_now = vop(role, eng, lambda e: e.bn_aggr(mv2b[:], bn6b[:]))
                            if role == "sc":
                                eng.wait_ge(vt.h, bn3_now)
                            sop(role, eng, lambda e: e.activation(lnvb[:], mv2b[:, 1:2], AF.Ln, bias=eps_c[:, 0:1]))
                            rs3_now = sop(role, eng, lambda e: e.activation(rstdb[:], lnvb[:], AF.Exp, scale=-0.5))
                            if role == "ve":
                                eng.wait_ge(stk.h, rs3_now)
                                if s >= 1:
                                    eng.wait_ge(s_tp2.h, s)
                            l3_h.append(vop(role, eng, lambda e, s=s: e.tensor_scalar(
                                xl3[:], x2_b[s % 2][:], mv2b[:, 0:1], rstdb[:, 0:1], ALU.subtract, ALU.mult)))
                            if role == "te":
                                eng.wait_ge(vt.h, l3_h[s])
                                if s >= 1:
                                    eng.wait_ge(stk.h, te2_h[s - 1])
                                for cs in range(NCS):
                                    p = psB if cs < 8 else psC
                                    mm = eng.transpose(p[:, (cs % 8) * 128:(cs % 8 + 1) * 128],
                                                       xl3[:, cs * 128:(cs + 1) * 128], id_s[:])
                                    if cs == NCS - 1:
                                        mm.then_inc(s_tp2.h, 1)
                            s_tp2.v += 1
                            if role == "sc":
                                eng.wait_ge(s_tp2.h, s_tp2.v)
                                s_3s.wait(eng, s - 1)
                            e_now = 0
                            for k in range(NK):
                                p = psB if k < 2 else psC
                                e_now = sop(role, eng, lambda e, k=k, s=s, p=p: e.copy(
                                    stg[s % 2][:, k * 512:(k + 1) * 512],
                                    p[:, (k % 2) * 512:(k % 2 + 1) * 512]))
                            te2_h.append(e_now)
                            if role == "sy":
                                eng.wait_ge(stk.h, te2_h[s])
                                s_3s.add(eng.dma_start(
                                    out=xln3T_sp.ap().rearrange("(c p) t -> p c t", p=128)[:, :, s * 128:(s + 1) * 128],
                                    in_=stg[s % 2][:].rearrange("p (c t) -> p c t", t=128),
                                ), s)
                            else:
                                s_3s.add(None, s)
                        endp(role, eng, {"sy": [s_xr, s_x2s, s_3s], "ve": vt, "te": s_tp2, "sc": stk})

                    run_phase(ph7)

        if upto >= 8:
            with ExitStack() as stD:
                sbD = lambda n, s, d: stD.enter_context(nc.sbuf_tensor(n, s, d))
                x3_b = [sbD(f"x3_b{i}", [128, NCS * 512], BF16) for i in range(2)]
                sd_bc = sbD("sd_bc", [128, D], F32)
                fuse = sbD("fuse", [128, cfg.n_ic * 512], BF16)
                wgu_b = [sbD(f"wgu_b{i}", [128, 2 * D], BF16) for i in range(3)]
                wd_b = [sbD(f"wd_b{i}", [128, 4 * 512], BF16) for i in range(4)]
                sil_b = [sbD(f"sil_b{i}", [128, 512], F32) for i in range(2)]
                ftok = sbD("ftok", [128, 4 * D], F32)
                x2r = [sbD(f"x2r{i}", [128, D], F32) for i in range(2)]
                bn6c = sbD("bn6c", [128, 6 * NK], F32)
                mv2c = sbD("mv2c", [128, 2], F32)
                lnvc = sbD("lnvc", [128, 1], F32)
                rstdc = sbD("rstdc", [128, 1], F32)

                s_f1g = sem("g_f1g")
                s_f1u = sem("g_f1u")
                s_f2m = sem("g_f2m")
                s_x3 = dpair("g_x3")
                s_wg = dpair("g_wg", n=3)
                s_wd = dpair("g_wd", n=4)
                s_x2r = dpair("g_x2r")
                s_ys = dpair("g_ys")

                def ph8(role, eng):
                    bc3_done = bcast_row(role, eng, d_sd, sd_bc, "g_b")
                    sil_h, fu_h, f2e_h, l4_h = [], [], [], []
                    def x3_load(eng, ttl):
                        return eng.dma_start(
                            out=x3_b[ttl % 2][:].rearrange("p (c t) -> p c t", t=512),
                            in_=xln3T_sp.ap().rearrange("(c p) t -> p c t", p=128)[:, :, ttl * 512:(ttl + 1) * 512],
                        )
                    for tt in range(cfg.n_tt):
                        if tt == 0:
                            s_x3.add(x3_load(eng, 0) if role == "sy" else None, 0)
                        if tt + 1 < cfg.n_tt:
                            if role == "sy":
                                if tt + 1 >= 2:
                                    eng.wait_ge(s_f1u.h, cfg.n_ic * tt)
                                s_x3.add(x3_load(eng, tt + 1), tt + 1)
                            else:
                                s_x3.add(None, tt + 1)
                        for ic in range(cfg.n_ic):
                            gic = tt * cfg.n_ic + ic
                            pre = (tt >= 1 and ic < 2)
                            if role == "gp":
                                if not pre:
                                    if gic >= 3:
                                        eng.wait_ge(s_f1u.h, gic - 2)
                                    s_wg.add(eng.dma_start(out=wgu_b[gic % 3][:], in_=d_wgu[ic, :, :],
                                                           max_dma_last_dim=2048), gic)
                            elif not pre:
                                s_wg.add(None, gic)
                            gsl = psS[:, (gic % 2) * 512:(gic % 2 + 1) * 512]
                            usl = psS[:, 1024 + (gic % 2) * 512: 1024 + (gic % 2 + 1) * 512]
                            if role == "te":
                                s_wg.wait(eng, gic)
                                if ic == 0:
                                    s_x3.wait(eng, tt)
                                    eng.wait_ge(stk.h, bc3_done)
                                    if tt >= 1:
                                        eng.wait_ge(vt.h, f2e_h[-1])
                                if gic >= 2:
                                    eng.wait_ge(stk.h, sil_h[gic - 2])
                                    eng.wait_ge(vt.h, fu_h[gic - 2])
                                for cs in range(NCS):
                                    mm = eng.matmul(gsl, wgu_b[gic % 3][:, cs * 128:(cs + 1) * 128],
                                                    x3_b[tt % 2][:, cs * 512:(cs + 1) * 512],
                                                    start=(cs == 0), stop=(cs == NCS - 1))
                                mm.then_inc(s_f1g.h, 1)
                                for cs in range(NCS):
                                    mm = eng.matmul(usl, wgu_b[gic % 3][:, D + cs * 128: D + (cs + 1) * 128],
                                                    x3_b[tt % 2][:, cs * 512:(cs + 1) * 512],
                                                    start=(cs == 0), stop=(cs == NCS - 1))
                                mm.then_inc(s_f1u.h, 1)
                            s_f1g.v += 1
                            s_f1u.v += 1
                            if role == "sc":
                                eng.wait_ge(s_f1g.h, s_f1g.v)
                                if gic >= 2:
                                    eng.wait_ge(vt.h, fu_h[gic - 2])
                            sil_h.append(sop(role, eng, lambda e, gic=gic, ic=ic, gsl=gsl: e.activation(
                                sil_b[gic % 2][:], gsl, AF.Silu, scale=sg_s[:, ic:ic + 1])))
                            if role == "ve":
                                eng.wait_ge(stk.h, sil_h[gic])
                                eng.wait_ge(s_f1u.h, s_f1u.v)
                            fu_h.append(vop(role, eng, lambda e, ic=ic, gic=gic, usl=usl: e.scalar_tensor_tensor(
                                fuse[:, ic * 512:(ic + 1) * 512], usl, su_s[:, ic:ic + 1],
                                sil_b[gic % 2][:], ALU.mult, ALU.mult)))
                        if tt + 1 < cfg.n_tt:
                            if role == "gp":
                                eng.wait_ge(s_f1u.h, cfg.n_ic * (tt + 1))
                                for icn in range(2):
                                    s_wg.add(eng.dma_start(out=wgu_b[((tt + 1) * cfg.n_ic + icn) % 3][:], in_=d_wgu[icn, :, :],
                                                           max_dma_last_dim=2048), (tt + 1) * cfg.n_ic + icn)
                            else:
                                for icn in range(2):
                                    s_wg.add(None, (tt + 1) * cfg.n_ic + icn)
                        dsl = [psB[:, 0:512], psB[:, 512:1024], psC[:, 0:512], psC[:, 512:1024]]
                        for oh in range(cfg.n_oh):
                            goh = tt * cfg.n_oh + oh
                            for grp in range(cfg.n_grp):
                                ggrp = goh * cfg.n_grp + grp
                                if role == "gp":
                                    if ggrp >= 4:
                                        eng.wait_ge(s_f2m.h, ggrp - 3)
                                    s_wd.add(eng.dma_start(
                                        out=wd_b[ggrp % 4][:].rearrange("p (n o) -> p n o", o=512),
                                        in_=d_wd[grp, :, :, oh * 512:(oh + 1) * 512],
                                    ), ggrp)
                                else:
                                    s_wd.add(None, ggrp)
                                if role == "te":
                                    s_wd.wait(eng, ggrp)
                                    if grp == 0:
                                        eng.wait_ge(vt.h, fu_h[-1])
                                        if goh >= 1:
                                            eng.wait_ge(vt.h, f2e_h[-1])
                                    for i4 in range(4):
                                        ic = grp * 4 + i4
                                        for t1 in range(4):
                                            mm = eng.matmul(dsl[t1],
                                                            fuse[:, ic * 512 + t1 * 128: ic * 512 + (t1 + 1) * 128],
                                                            wd_b[ggrp % 4][:, i4 * 512:(i4 + 1) * 512],
                                                            start=(grp == 0 and i4 == 0),
                                                            stop=(grp == cfg.n_grp - 1 and i4 == 3))
                                    mm.then_inc(s_f2m.h, 1)
                                s_f2m.v += 1
                            if role == "ve":
                                eng.wait_ge(s_f2m.h, s_f2m.v)
                                if oh == 0 and tt >= 1:
                                    for t1 in range(4):
                                        s_ys.wait(eng, (tt - 1) * 4 + t1)
                            e_now = 0
                            for t1 in range(4):
                                e_now = vop(role, eng, lambda e, t1=t1, oh=oh: e.scalar_tensor_tensor(
                                    ftok[:, t1 * D + oh * 512: t1 * D + (oh + 1) * 512],
                                    dsl[t1], 1.0, sd_bc[:, oh * 512:(oh + 1) * 512], ALU.mult, ALU.mult))
                            f2e_h.append(e_now)
                        for t1 in range(4):
                            gt = tt * 4 + t1
                            if role == "sy":
                                if gt >= 2:
                                    eng.wait_ge(vt.h, l4_h[gt - 2])
                                s_x2r.add(eng.dma_start(out=x2r[gt % 2][:],
                                                        in_=x2_sp.ap()[gt * 128:(gt + 1) * 128, :]), gt)
                            else:
                                s_x2r.add(None, gt)
                            for k in range(NK):
                                vop(role, eng, lambda e, k=k, t1=t1: e.bn_stats(
                                    bn6c[:, 6 * k:6 * (k + 1)], ftok[:, t1 * D + 512 * k: t1 * D + 512 * (k + 1)]))
                            bn4_now = vop(role, eng, lambda e: e.bn_aggr(mv2c[:], bn6c[:]))
                            if role == "sc":
                                eng.wait_ge(vt.h, bn4_now)
                            sop(role, eng, lambda e: e.activation(lnvc[:], mv2c[:, 1:2], AF.Ln, bias=eps_c[:, 0:1]))
                            rs4_now = sop(role, eng, lambda e: e.activation(rstdc[:], lnvc[:], AF.Exp, scale=-0.5))
                            if role == "ve":
                                eng.wait_ge(stk.h, rs4_now)
                            if role == "ve":
                                s_ys.wait(eng, gt - 2)
                            vop(role, eng, lambda e, t1=t1: e.tensor_scalar(
                                ftok[:, t1 * D:(t1 + 1) * D], ftok[:, t1 * D:(t1 + 1) * D],
                                mv2c[:, 0:1], rstdc[:, 0:1], ALU.subtract, ALU.mult))
                            if role == "ve":
                                s_x2r.wait(eng, gt)
                            l4_h.append(vop(role, eng, lambda e, t1=t1, gt=gt: e.tensor_tensor(
                                ftok[:, t1 * D:(t1 + 1) * D], ftok[:, t1 * D:(t1 + 1) * D],
                                x2r[gt % 2][:], ALU.add)))
                            if role == "sy":
                                eng.wait_ge(vt.h, l4_h[gt])
                                s_ys.add(eng.dma_start(out=d_y[gt * 128:(gt + 1) * 128, :],
                                                       in_=ftok[:, t1 * D:(t1 + 1) * D]), gt)
                            else:
                                s_ys.add(None, gt)
                    endp(role, eng, {"sy": [s_x3, s_x2r, s_ys], "gp": [s_wg, s_wd], "ve": vt,
                                     "te": s_f2m, "sc": stk})

                run_phase(ph8)

        if dbg_pairs:
            def phdbg(role, eng):
                s_d = sem("z_dbg")
                if role == "gp":
                    for dst, srct in dbg_pairs:
                        eng.dma_start(out=dst, in_=srct.ap()).then_inc(s_d.h, 16)
                s_d.v += 16 * len(dbg_pairs)
                if role == "gp":
                    eng.wait_ge(s_d.h, s_d.v)
                endp(role, eng, {})

            run_phase(phdbg)

    return nc



def _host_prep(cfg, inputs):
    D, To, T = cfg.D, cfg.To, cfg.T_seq
    w_qkv = np.asarray(inputs["L0_attn_w_qkv"])
    b_qkv = np.asarray(inputs["L0_attn_b_qkv"])
    w_out = np.asarray(inputs["L0_attn_w_out"])
    b_out = np.asarray(inputs["L0_attn_b_out"])
    w_gate = np.asarray(inputs["L0_ffn_w_gate"])
    w_up = np.asarray(inputs["L0_ffn_w_up"])
    w_down = np.asarray(inputs["L0_ffn_w_down"])
    s_gate = np.asarray(inputs["L0_ffn_s_gate"])
    s_up = np.asarray(inputs["L0_ffn_s_up"])
    s_down = np.asarray(inputs["L0_ffn_s_down"])

    heads = cfg.heads
    perm = np.concatenate([np.arange(0, 128, 2), np.arange(1, 128, 2)])
    rows = (np.arange(heads)[:, None] * 128 + perm[None, :]).ravel()
    wq = w_qkv[rows, :]
    wk = w_qkv[D + rows, :]
    wqk = np.concatenate([wq, wk], axis=0)
    d_wqk = np.ascontiguousarray(
        wqk.reshape(cfg.n_fc, 128, cfg.n_csub, 128).transpose(0, 3, 2, 1).reshape(cfg.n_fc, 128, D))
    bqk = np.concatenate([b_qkv[rows], b_qkv[D + rows]])
    d_bqk = np.ascontiguousarray(bqk.reshape(cfg.n_fc, 128).T)
    d_wv = np.ascontiguousarray(w_qkv[2 * D:3 * D, :].T.reshape(cfg.n_csub, 128, D))
    d_bv = np.ascontiguousarray(b_qkv[2 * D:3 * D].reshape(1, D))
    d_wo = np.ascontiguousarray(w_out.T.reshape(cfg.n_csub, 128, D))
    d_bo = np.ascontiguousarray(b_out.reshape(1, D))
    d_wg = w_gate.reshape(cfg.n_csub, 128, cfg.n_ic, 128).transpose(2, 1, 0, 3).reshape(cfg.n_ic, 128, D)
    d_wu = w_up.reshape(cfg.n_csub, 128, cfg.n_ic, 128).transpose(2, 1, 0, 3).reshape(cfg.n_ic, 128, D)
    d_wgu = np.ascontiguousarray(np.concatenate([d_wg, d_wu], axis=2))
    d_wd = np.ascontiguousarray(
        w_down.reshape(cfg.n_grp, 4, 128, D).transpose(0, 2, 1, 3))
    d_sg = np.ascontiguousarray(s_gate.reshape(cfg.n_ic, 128).T).astype(np.float32)
    d_su = np.ascontiguousarray(s_up.reshape(cfg.n_ic, 128).T).astype(np.float32)
    d_sd = np.ascontiguousarray(s_down.reshape(1, D)).astype(np.float32)

    j = np.arange(64, dtype=np.float64)
    theta = 10000.0 ** (-j / 64.0)
    mw = cfg.QW + (cfg.ND - 1) * 128
    si = np.arange(128)[:, None]
    v = np.arange(mw)[None, :]
    d_mask = (si <= (v - (cfg.ND - 1) * 128)).astype(np.float32).astype(BF)

    shared = dict(wqk=d_wqk, bqk=d_bqk, wv=d_wv, bv=d_bv, wo=d_wo, bo=d_bo,
                  wgu=d_wgu, wd=d_wd, sg=d_sg, su=d_su, sd=d_sd, maskP=d_mask)

    x_full = np.asarray(inputs["L0_attn_x"])
    in_maps = []
    for c in range(8):
        b, g = c // 4, c % 4
        pos = np.arange(g * To, (g + 1) * To, dtype=np.float64)
        ang = pos[None, :] * theta[:, None]
        cos64 = np.cos(ang)
        sin64 = np.sin(ang)
        cosP = np.concatenate([cos64, cos64], axis=0).astype(BF)
        sinP = np.concatenate([-sin64, sin64], axis=0).astype(BF)
        m = dict(shared)
        m["x"] = np.ascontiguousarray(x_full[b, g * To:(g + 1) * To, :])
        m["gix"] = np.array([[g]], dtype=np.int32)
        m["identF"] = np.eye(128, dtype=np.float32)
        m["cosP"] = np.ascontiguousarray(cosP)
        m["sinP"] = np.ascontiguousarray(sinP)
        in_maps.append(m)
    return in_maps


def kernel(**inputs):
    cfg = Cfg()
    in_maps = _host_prep(cfg, inputs)
    nc = build(cfg)
    res = run_bass_kernel_spmd(nc, in_maps, core_ids=list(range(8)))
    B, T, D = 2, cfg.T_seq, cfg.D
    y = np.empty((B, T, D), dtype=np.float32)
    To = cfg.To
    for c in range(8):
        b, g = c // 4, c % 4
        y[b, g * To:(g + 1) * To, :] = np.asarray(res.results[c]["y"], dtype=np.float32)
    return y
